# version 17 (final)
# baseline (speedup 1.0000x reference)
import math
import numpy as np
import ml_dtypes

import concourse.bass as bass
import concourse.mybir as mybir
from concourse.bass_utils import run_bass_kernel_spmd

F32 = mybir.dt.float32
BF16 = mybir.dt.bfloat16
U32 = mybir.dt.uint32
AF = mybir.ActivationFunctionType
ALU = mybir.AluOpType
ET = mybir.EngineType

B, T, C, H = 4, 4096, 4096, 32
TOK = 2048
NCC = 32
NOC = 32
SCALE = 1.0 / math.sqrt(128.0)
EXP_BIAS = -24.0
LN_EPS = 1e-5
bfdt = ml_dtypes.bfloat16


class Sem:
    def __init__(self, nc, name):
        self.h = nc.alloc_semaphore(name)
        self.n = 0

    def bump(self, k):
        self.n += k
        return self.n


def _build_nc():
    nc = bass.Bass("TRN2", target_bir_lowering=False,
                   detect_race_conditions=False)

    def din(name, shape, dt=F32):
        return nc.dram_tensor(name, list(shape), dt, kind="ExternalInput").ap()

    xT_p = din("xT_p", [NCC, 128, TOK])
    wq_p = din("wq_p", [NOC, 128, NCC, 128])
    wk_p = din("wk_p", [NOC, 128, NCC, 128])
    wv_p = din("wv_p", [NOC, 128, NCC, 128])
    wo_p = din("wo_p", [NOC, 128, NCC, 128])
    W_p = din("W_p", [NOC, 128, NCC, 128])
    A2_p = din("A2_p", [2, 128, 16, 256])
    Bf_p = din("Bf_p", [128, 2, C])
    bias_in = din("bias_in", [1, C])
    coef_in = din("coef_in", [16, 1])
    scal_in = din("scal_in", [16, 1])
    m16_in = din("m16_in", [16, 256])
    cos_in = din("cos_in", [128, TOK])
    sal_in = din("sal_in", [128, TOK])
    mask_in = din("mask_in", [128, 4096], BF16)
    ident_in = din("ident_in", [128, 128], BF16)
    nonce_in = din("nonce_in", [1, 1], U32)

    out_d = nc.dram_tensor("out_fm", [C, TOK], F32, kind="ExternalOutput").ap()

    qtd = nc.dram_tensor("qtd", [C, TOK], BF16, kind="Internal").ap()
    x1d = nc.dram_tensor("x1d", [NCC, 128, TOK], BF16, kind="Internal").ap()
    kloc = nc.dram_tensor("kloc", [C, TOK], BF16, kind="Internal").ap()
    vloc = nc.dram_tensor("vloc", [H, 128, 16, 128], BF16,
                          kind="Internal").ap()
    shK = nc.dram_tensor("shK", [2, C, TOK], BF16, kind="Internal",
                         addr_space="Shared").ap()
    shV = nc.dram_tensor("shV", [H, 128, 32, 128], BF16, kind="Internal",
                         addr_space="Shared").ap()
    shF = nc.dram_tensor("shF", [2, 16], U32, kind="Internal",
                         addr_space="Shared").ap()

    from contextlib import ExitStack
    ctx = ExitStack()
    e = ctx.enter_context

    arena = e(nc.sbuf_tensor("arena", [128, 65536], BF16))
    slab0 = e(nc.sbuf_tensor("slab0", [128, 4096], BF16))
    slab1 = e(nc.sbuf_tensor("slab1", [128, 4096], BF16))
    slabs = [slab0, slab1]
    poolb = e(nc.sbuf_tensor("poolb", [128, 5120], BF16))
    poolc = e(nc.sbuf_tensor("poolc", [128, 8192], BF16))
    poolf = e(nc.sbuf_tensor("poolf", [128, 2048], F32))
    scr1 = e(nc.sbuf_tensor("scr1", [128, 1024], BF16))
    scr2 = e(nc.sbuf_tensor("scr2", [128, 1024], BF16))
    scr3 = e(nc.sbuf_tensor("scr3", [128, 1024], BF16))
    vtb = e(nc.sbuf_tensor("vtb", [128, 2048], BF16))
    biasb = e(nc.sbuf_tensor("biasb", [1, 4096], BF16))
    bcb = e(nc.sbuf_tensor("bcb", [128, 2048], F32))
    onesrow = e(nc.sbuf_tensor("onesrow", [1, 512], BF16))
    pmr1f = e(nc.sbuf_tensor("pmr1f", [1, 256], F32))
    onesr1f = pmr1f[:, 0:128]
    negr1f = pmr1f[:, 128:256]
    onescol = e(nc.sbuf_tensor("onescol", [128, 1], BF16))
    onescolf = e(nc.sbuf_tensor("onescolf", [128, 1], F32))
    m16x = e(nc.sbuf_tensor("m16x", [16, 258], BF16))
    m16b = m16x[:, 0:256]
    w16c = m16x[:, 256:257]
    idb = e(nc.sbuf_tensor("idb", [128, 128], BF16))
    wppb4 = e(nc.sbuf_tensor("wppb4", [128, 2], F32))
    wppb = wppb4[:, 0:2]
    cf32 = wppb4[0:16, 0:2]
    cstf = e(nc.sbuf_tensor("cstf", [128, 4], F32))
    nc.const_aps.register(F32, EXP_BIAS, cstf[:, 0:1])
    nc.const_aps.register(F32, LN_EPS, cstf[:, 1:2])
    nc.const_aps.register(F32, SCALE, cstf[:, 2:3])
    nc.const_aps.register(F32, -0.5, cstf[:, 3:4])
    nc.const_aps.register(F32, -1.0, negr1f[:, 0:1])
    nfb = e(nc.sbuf_tensor("nfbuf", [1, 8], U32))
    nb = nfb[:, 0:1]
    fb = nfb[:, 4:5]

    qc_s = [poolf[:, 0:512], poolf[:, 512:1024]]
    qs_s = [poolf[:, 1024:1536], poolf[:, 1536:2048]]
    Rs = poolf[:, 0:512]
    rl = poolf[0:1, 512:1024]
    rlg = poolf[0:1, 0:512]
    xr_s = [poolf[:, i * 512:(i + 1) * 512] for i in range(4)]
    ob_s = [poolf[:, 0:1024], poolf[:, 1024:2048]]
    strow = [poolf[0:1, 0:1024], poolf[32:33, 0:1024],
             poolf[32:33, 1024:2048], poolf[0:1, 1024:2048]]
    qts = [poolb[:, 0:2048], poolb[:, 2048:4096]]
    ptil = [poolb[:, 4096:4608], poolb[:, 4608:5120], scr3[:, 0:512]]
    cosb = poolc[:, 0:2048]
    salb = poolc[:, 2048:4096]
    vbufB = poolc[:, 0:4096]
    maskb = poolc[:, 4096:8192]
    BfT = poolc
    hT = vtb
    lacc = [bcb[:, 0:512], bcb[:, 512:1024]]
    laccb = poolf[:, 1024:1280].bitcast(BF16)

    ps = [e(nc.psum_tensor(f"ps{i}", [128, 512], F32)) for i in range(7)]
    psT = e(nc.psum_tensor("psT", [128, 512], BF16))
    psT2 = ps[6][:, 0:256].bitcast(BF16)

    S = lambda name: Sem(nc, name)
    s_cst = S("cst"); s_csh = S("csh"); s_vst = S("vst"); s_wpp = S("wpp")
    s_xld = S("xld"); s_wsl = S("wsl")
    s_mm1 = S("mm1"); s_act1 = S("act1")
    s_dve1 = S("dve1"); s_pl1 = S("pl1"); s_st1 = S("st1")
    s_vtr = S("vtr"); s_vev = S("vev"); s_st2 = S("st2")
    s_fl = S("fl"); s_x = S("x"); s_lc = S("lc")
    s_kld = S("kld"); s_sc = S("sc"); s_ex = S("ex")
    s_pv = S("pv"); s_rc = S("rc"); s_rm = S("rm"); s_rn = S("rn")
    s_lv = S("lv"); s_ls = S("ls")
    s_p3l = S("p3l"); s_p3m = S("p3m"); s_x1 = S("x1"); s_x1st = S("x1st")
    s_a2l = S("a2l"); s_bfl = S("bfl")
    s_rl4 = S("rl4"); s_sq4 = S("sq4"); s_sf4 = S("sf4"); s_nm4 = S("nm4")
    s_hm4 = S("hm4"); s_he4 = S("he4"); s_mm4 = S("mm4"); s_x2 = S("x2")
    s_sq5 = S("sq5"); s_sf5 = S("sf5"); s_no5 = S("no5"); s_out = S("out")

    ops = {k: [] for k in ("sync", "gpsimd", "tensor", "vector", "scalar")}
    rt = {}

    def add(eng, fn):
        ops[eng].append(fn)

    def dma(eng, out, in_, sem, waits=(), n=16):
        wl = list(waits)
        if eng == "gpsimd" and sem.n:
            wl.append((sem, sem.n))
        tgt = sem.bump(n)

        def fn(en, out=out, in_=in_, wl=wl, sem=sem, n=n):
            for s, v in wl:
                if v:
                    en.wait_ge(s.h, v)
            en.dma_start(out=out, in_=in_).then_inc(sem.h, n)
        add(eng, fn)
        return tgt

    def step(eng, waits, emit, sem=None, n=1, sem2=None):
        tgt = sem.bump(n) if sem is not None else None
        if sem2 is not None:
            sem2.bump(1)
        wl = list(waits)

        def fn(en, wl=wl, emit=emit, sem=sem, n=n, sem2=sem2):
            for s, v in wl:
                if v:
                    en.wait_ge(s.h, v)
            ins = emit(en)
            if sem is not None:
                ins = ins.then_inc(sem.h, n)
            if sem2 is not None:
                en.nop().then_inc(sem2.h, 1)
        add(eng, fn)
        return tgt

    dma("gpsimd", biasb[:], bias_in, s_cst)
    dma("gpsimd", cosb, cos_in, s_cst)
    dma("gpsimd", salb, sal_in, s_cst)
    dma("gpsimd", m16b[:], m16_in, s_cst)
    dma("sync", idb[:], ident_in, s_csh)
    dma("sync", maskb, mask_in, s_csh)
    dma("sync", nb[:], nonce_in, s_csh)
    dma("gpsimd", cf32[:, 0:1], coef_in, s_cst)
    dma("gpsimd", cf32[:, 1:2], scal_in, s_cst)
    cst_all = s_cst.n
    csh_all = s_csh.n

    def consts_v(v):
        v.wait_ge(s_cst.h, cst_all)
        v.wait_ge(s_csh.h, csh_all)
        v.memset(onescol[:], 1.0)
        v.memset(onescolf[:], 1.0)
        v.memset(cstf[:, 0:1], EXP_BIAS)
        v.memset(cstf[:, 1:2], LN_EPS)
        v.memset(cstf[:, 2:3], SCALE)
        v.memset(cstf[:, 3:4], -0.5)
        v.memset(onesrow[:], 1.0)
        v.memset(onesr1f[:], 1.0)
        v.memset(negr1f[:], -1.0)
        v.tensor_mul(w16c[:], cf32[:, 0:1], cf32[:, 1:2]).then_inc(s_vst.h, 1)
    add("vector", consts_v)
    s_vst.bump(1)

    for et in range(2):
        step("tensor", [(s_vst, 1), (s_wpp, s_wpp.n)],
             lambda t, et=et: t.matmul(ps[6][:, 0:1],
                                       m16b[:, et * 128:(et + 1) * 128],
                                       w16c[:], start=True, stop=True),
             s_wpp)
        step("vector", [(s_wpp, s_wpp.n)],
             lambda v, et=et: v.tensor_copy(wppb[:, et:et + 1], ps[6][:, 0:1]),
             s_wpp)
    wpp_all = s_wpp.n

    dma("gpsimd", arena[:].rearrange("p (c t) -> p c t", t=TOK),
        xT_p.rearrange("c p t -> p c t"), s_xld)
    xld_all = s_xld.n

    wsrc = {"q": wq_p, "k": wk_p, "v": wv_p}
    kinds = [("q", oc) for oc in range(NOC)] + \
            [("k", oc) for oc in range(NOC)] + \
            [("v", oc) for oc in range(NOC)]

    slab_free = [0, 0]
    bank_free = [0] * 6
    r_mul1 = [0, 0]
    r_pool = [0, 0]
    r_store = [0, 0]
    v_store = 0
    vev_bank = [0, 0]
    gtile = 0

    w_tgts = {}
    kind0, oc0 = kinds[0]
    w_tgts[0] = dma("gpsimd", slabs[0][:], wsrc[kind0][oc0], s_wsl,
                    waits=[(s_cst, cst_all)])

    pend_vtrs = []
    pend_vvs = []
    last_vvf_sl = {0: 0, 1: 0}
    st2_by_oc = {}
    v_store = 0
    for ki, (kind, oc) in enumerate(kinds):
        par = ki % 2
        if kind == "v" and oc == 0:
            st1_qk = s_st1.n
            dve1_v0 = s_dve1.n

            def pubk(s, st1_qk=st1_qk):
                s.wait_ge(s_st1.h, st1_qk)
                role = rt["sync_role"]
                for i in range(8):
                    s.dma_start(
                        out=shK[bass.ts(role, 1), i * 512:(i + 1) * 512, :],
                        in_=kloc[i * 512:(i + 1) * 512, :]).then_inc(
                        s_fl.h, 16)
            add("sync", pubk)
            s_fl.bump(128)
        if ki + 1 < len(kinds):
            k2, o2 = kinds[ki + 1]
            w_tgts[ki + 1] = dma("gpsimd", slabs[(ki + 1) % 2][:],
                                 wsrc[k2][o2], s_wsl,
                                 waits=[(s_mm1, slab_free[(ki + 1) % 2])])

        bks = [(ki * 4 + b) % 6 for b in range(4)]

        def mmblk(t, par=par, w_tgt=w_tgts[ki], bks=tuple(bks),
                  bf=tuple(bank_free[bk] for bk in bks)):
            t.wait_ge(s_wsl.h, w_tgt)
            t.wait_ge(s_xld.h, xld_all)
            for b in range(4):
                if bf[b]:
                    t.wait_ge(s_act1.h, bf[b])
                for c in range(NCC):
                    lhsT = slabs[par][:, c * 128:(c + 1) * 128]
                    rhs = arena[:, c * 2048 + b * 512: c * 2048 + b * 512 + 512]
                    ins = t.matmul(ps[bks[b]][:], lhsT, rhs,
                                   start=(c == 0), stop=(c == NCC - 1))
                ins.then_inc(s_mm1.h, 1)
        add("tensor", mmblk)
        mm_base = s_mm1.n
        mm_tgt = s_mm1.bump(4)
        slab_free[par] = mm_tgt
        for f_ in pend_vtrs:
            add("tensor", f_)
        pend_vtrs = []
        vv_flush = pend_vvs
        pend_vvs = []

        for f_ in vv_flush:
            add("scalar", f_)
        vv_flush = []
        if kind in ("q", "k"):
            for b in range(4):
                tp = gtile % 2
                act_tgt = step(
                    "scalar",
                    [(s_mm1, mm_base + b + 1), (s_dve1, r_mul1[tp])],
                    lambda sc, bk=bks[b], tp=tp: sc.copy(qc_s[tp], ps[bk][:]),
                    s_act1)
                bank_free[bks[b]] = act_tgt

                shmask = [i ^ 1 for i in range(32)]
                sh_tgt = step(
                    "vector", [(s_act1, act_tgt)],
                    lambda v, tp=tp, shmask=shmask: v.stream_shuffle(
                        qs_s[tp], qc_s[tp], shmask),
                    s_dve1)
                m1_tgt = step(
                    "vector", [(s_pl1, r_pool[tp])],
                    lambda v, b=b, tp=tp: v.tensor_mul(
                        scr1[:, tp * 512:(tp + 1) * 512], qc_s[tp],
                        cosb[:, b * 512:(b + 1) * 512]),
                    s_dve1)
                r_mul1[tp] = m1_tgt
                p2_tgt = step(
                    "gpsimd", [(s_dve1, sh_tgt)],
                    lambda g, b=b, tp=tp: g.tensor_mul(
                        scr2[:, tp * 512:(tp + 1) * 512], qs_s[tp],
                        salb[:, b * 512:(b + 1) * 512]),
                    s_pl1)
                r_pool[tp] = p2_tgt
                ad_tgt = step(
                    "vector",
                    [(s_pl1, p2_tgt), (s_st1, r_store[tp])],
                    lambda v, tp=tp: v.tensor_add(
                        scr3[:, tp * 512:(tp + 1) * 512],
                        scr1[:, tp * 512:(tp + 1) * 512],
                        scr2[:, tp * 512:(tp + 1) * 512]),
                    s_dve1)
                if kind == "q":
                    st_tgt = dma(
                        "sync",
                        qtd[oc * 128:(oc + 1) * 128, b * 512:(b + 1) * 512],
                        scr3[:, tp * 512:(tp + 1) * 512], s_st1,
                        waits=[(s_dve1, ad_tgt)])
                else:
                    st_tgt = dma(
                        "sync",
                        kloc[oc * 128:(oc + 1) * 128, b * 512:(b + 1) * 512],
                        scr3[:, tp * 512:(tp + 1) * 512], s_st1,
                        waits=[(s_dve1, ad_tgt)])
                r_store[tp] = st_tgt
                gtile += 1
        else:
            vtr_base = s_vtr.n
            st_prev = v_store
            for b in range(4):
                evs = [scr1[:, 0:512], scr1[:, 512:1024],
                       scr2[:, 0:512], scr2[:, 512:1024]][b]
                evw = [(s_mm1, mm_base + b + 1), (s_vtr, vtr_base)]
                if vtr_base == 0:
                    evw.append((s_dve1, dve1_v0))
                ev_tgt = step(
                    "scalar", evw,
                    lambda sc, bk=bks[b], evs=evs: sc.copy(evs, ps[bk][:]),
                    s_act1)
                bank_free[bks[b]] = ev_tgt

                pst = psT if b % 2 == 0 else psT2

                def vtr_blk(t, evs=evs, ev_tgt=ev_tgt, pst=pst,
                            vev_need=last_vvf_sl[b % 2]):
                    t.wait_ge(s_act1.h, ev_tgt)
                    if vev_need:
                        t.wait_ge(s_vev.h, vev_need)
                    for sub in range(4):
                        ins = t.transpose(
                            pst[:, sub * 128:(sub + 1) * 128],
                            evs[:, sub * 128:(sub + 1) * 128],
                            idb[:])
                    ins.then_inc(s_vtr.h, 1)
                pend_vtrs.append(vtr_blk)
                vtr_tgt = s_vtr.bump(1)

                def vvf(sc, b=b, pst=pst, vtr_tgt=vtr_tgt,
                        st_prev=st_prev):
                    sc.wait_ge(s_vtr.h, vtr_tgt)
                    if st_prev:
                        sc.wait_ge(s_st2.h, st_prev)
                    sc.copy(vtb[:, b * 512:(b + 1) * 512],
                            pst[:]).then_inc(s_vev.h, 1)
                pend_vvs.append(vvf)
                vv_tgt = s_vev.bump(1)
                last_vvf_sl[b % 2] = vv_tgt

            v_store = dma("sync", vloc[oc], vtb[:, 0:2048], s_st2,
                          waits=[(s_vev, vv_tgt)])
            st2_by_oc[oc] = v_store
            if oc % 4 == 3:
                def pubvc(s, tgt=v_store, hc=oc // 4):
                    s.wait_ge(s_st2.h, tgt)
                    role = rt["sync_role"]
                    s.dma_start(
                        out=shV[hc * 4:(hc + 1) * 4, :,
                                bass.ts(role, 16), :],
                        in_=vloc[hc * 4:(hc + 1) * 4]).then_inc(s_fl.h, 16)
                add("sync", pubvc)
                s_fl.bump(16)

    for f_ in pend_vtrs:
        add("tensor", f_)
    pend_vtrs = []
    for f_ in pend_vvs:
        add("scalar", f_)
    pend_vvs = []
    st1_all = s_st1.n
    st2_all = s_st2.n
    mm1_all = s_mm1.n

    fl_pub = s_fl.n

    def flagw(s, fl_pub=fl_pub):
        s.wait_ge(s_fl.h, fl_pub)
        role = rt["sync_role"]
        s.dma_start(out=shF[bass.ts(role, 1), 0:1], in_=nb[:]).then_inc(
            s_fl.h, 16)
    add("sync", flagw)
    fl_all = s_fl.bump(16)

    def poll(g, fl_all=fl_all):
        g.wait_ge(s_fl.h, fl_all)
        opp = rt["gp_opp"]
        ms = nc.monotonic_semaphore(0, engine=g)
        with (g.register("rn_") as rn_, g.register("rf_") as rf_,
              g.register("rc_") as rc_):
            g.load(rn_, nb[0:1, 0:1])
            g.reg_mov(rc_, 1)
            with g.While(rc_):
                g.dma_start(out=fb[:], in_=shF[bass.ts(opp, 1), 0:1]
                            ).then_inc(ms.sem(), 16)
                ms.wait_inc(16)
                g.load(rf_, fb[0:1, 0:1])
                g.reg_alu(rc_, rf_, rn_, ALU.not_equal)
        g.sem_inc(s_x.h, 1)
    add("gpsimd", poll)
    s_x.bump(1)

    sc_after = [0] * (H + 1)
    pv_after = [0] * (H + 1)
    exp_free = [0, 0, 0]
    sbank = [ps[0], ps[1], ps[5]]
    pv_free = [0, 0, 0]
    lv_free = [0, 0, 0]
    ls_free = [0, 0]
    on_free = [0, 0]
    rm_prev = 0
    rn_prev = 0
    pend_pv = []
    pend_rm = None
    pend_rc = None
    rc_prev = 0
    lc_prev = 0

    for h in range(H):
        hb = h % 2
        vb = slab1[:] if hb == 0 else vbufB
        vw = [(s_x, 1), (s_csh, csh_all if h == 0 else 0)]
        if h >= 2:
            vw.append((s_pv, pv_after[h - 2]))
        if h == 1:
            vw.append((s_pl1, s_pl1.n))
        dma("sync", vb, shV[h], s_kld, waits=vw)
        qw = [(s_sc, sc_after[h - 2])] if h >= 2 else []
        dma("sync", qts[hb], qtd[h * 128:(h + 1) * 128, :], s_kld, waits=qw)
        kw = [(s_sc, sc_after[h - 1])] if h >= 1 else []
        for rr in range(2):
            for kc in range(4):
                dma("sync",
                    slab0[:, rr * 2048 + kc * 512:rr * 2048 + (kc + 1) * 512],
                    shK[rr, h * 128:(h + 1) * 128, kc * 512:(kc + 1) * 512],
                    s_kld, waits=kw)
                kw = []
        kld_need = s_kld.n

        for qt in range(4):
            nk = 8 * qt + 8
            opar = (h * 4 + qt) % 2
            bo = ps[2 + opar]
            bl = ps[4]
            for j in range(nk):
                ab = (s_sc.n) % 3
                pb = (s_sc.n) % 3

                def scmm(t, hb=hb, qt=qt, j=j, ab=ab, kld_need=kld_need,
                         exf=exp_free[ab],
                         onf=(on_free[opar] if j == 0 else 0)):
                    t.wait_ge(s_kld.h, kld_need)
                    if exf:
                        t.wait_ge(s_ex.h, exf)
                    if onf:
                        t.wait_ge(s_rn.h, onf)
                    col = (j % 2) * 2048 + (j // 2) * 128
                    d = j - 8 * qt
                    ba = sbank[ab]
                    lhsT = slab0[:, col:col + 128]
                    rhs = qts[hb][:, qt * 512:(qt + 1) * 512]
                    if d < 0:
                        t.matmul(ba[:], lhsT, rhs, start=True, stop=True
                                 ).then_inc(s_sc.h, 1)
                    else:
                        t.matmul(ba[:], lhsT, rhs, start=True, stop=False)
                        t.matmul(ba[:], idb[:],
                                 maskb[:, d * 512:(d + 1) * 512],
                                 start=False, stop=True).then_inc(s_sc.h, 1)
                add("tensor", scmm)
                sc_tgt = s_sc.bump(1)
                if len(pend_pv) >= 2:
                    add("tensor", pend_pv.pop(0))
                if j == 2 and pend_rc is not None:
                    add("scalar", pend_rc)
                    pend_rc = None
                if j == 3 and pend_rm is not None:
                    add("tensor", pend_rm)
                    pend_rm = None

                ex_tgt = step(
                    "scalar",
                    [(s_sc, sc_tgt), (s_pv, pv_free[pb]),
                     (s_lv, lv_free[pb])],
                    lambda sc, ab=ab, pb=pb: sc.activation(
                        ptil[pb], sbank[ab][:], AF.Exp,
                        bias=EXP_BIAS, scale=SCALE),
                    s_ex)
                exp_free[ab] = ex_tgt

                def pvmm(t, j=j, pb=pb, ex_tgt=ex_tgt, bo=bo,
                         vb=vb, first=(j == 0), last=(j == nk - 1)):
                    t.wait_ge(s_ex.h, ex_tgt)
                    sl = (j % 2) * 16 + (j // 2)
                    t.matmul(bo[:], vb[:, sl * 128:(sl + 1) * 128],
                             ptil[pb], start=first, stop=last
                             ).then_inc(s_pv.h, 1)
                pend_pv.append(pvmm)
                pv_free[pb] = s_pv.bump(1)
                lw = [(s_ex, ex_tgt)]
                if j == 0:
                    lw.append((s_ls, ls_free[opar]))
                if j == 0:
                    lv_free[pb] = step(
                        "vector", lw,
                        lambda v, pb=pb, opar=opar: v.tensor_copy(
                            lacc[opar], ptil[pb]),
                        s_lv)
                else:
                    lv_free[pb] = step(
                        "vector", lw,
                        lambda v, pb=pb, opar=opar: v.tensor_add(
                            lacc[opar], lacc[opar], ptil[pb]),
                        s_lv)

            while pend_pv:
                add("tensor", pend_pv.pop(0))
            lv_need = s_lv.n

            lcw = [(s_lv, lv_need), (s_ls, lc_prev)]
            if lc_prev == 0:
                lcw.append((s_pl1, s_pl1.n))
            lc_tgt = step(
                "vector", lcw,
                lambda v, opar=opar: v.tensor_copy(laccb, lacc[opar]),
                s_lc)

            def lsum(t, lc_tgt=lc_tgt, bl=bl,
                     rc_prev=rc_prev):
                t.wait_ge(s_lc.h, lc_tgt)
                if rc_prev:
                    t.wait_ge(s_rc.h, rc_prev)
                t.matmul(bl[0:1, :], onescol[:], laccb,
                         start=True, stop=True).then_inc(s_ls.h, 1)
            add("tensor", lsum)
            ls_tgt = s_ls.bump(1)
            ls_free[opar] = ls_tgt
            lc_prev = ls_tgt

            def rcf(sc, ls_tgt=ls_tgt, rm_prev=rm_prev, rn_prev=rn_prev,
                    bl=bl):
                sc.wait_ge(s_ls.h, ls_tgt)
                if rm_prev:
                    sc.wait_ge(s_rm.h, rm_prev)
                if rn_prev:
                    sc.wait_ge(s_rn.h, rn_prev)
                sc.activation(rlg, bl[0:1, :], AF.Ln)
                sc.activation(rl, rlg, AF.Exp, scale=-1.0).then_inc(
                    s_rc.h, 1)
            pend_rc = rcf
            rc_tgt = s_rc.bump(1)
            rc_prev = rc_tgt

            def rmmm(t, rc_tgt=rc_tgt, rn_prev=rn_prev):
                t.wait_ge(s_rc.h, rc_tgt)
                t.wait_ge(s_wpp.h, wpp_all)
                if rn_prev:
                    t.wait_ge(s_rn.h, rn_prev)
                t.matmul(ps[6][:], onesr1f[:], rl,
                         start=True, stop=True).then_inc(s_rm.h, 1)
            pend_rm = rmmm
            rm_tgt = s_rm.bump(1)
            rm_prev = rm_tgt
            rn_tgt = step(
                "vector", [(s_rm, rm_tgt)],
                lambda v, h=h, qt=qt, bo=bo: (
                    v.tensor_copy(Rs, ps[6][:]),
                    v.tensor_mul(
                        arena[:, h * 2048 + qt * 512:
                              h * 2048 + (qt + 1) * 512],
                        bo[:], Rs))[-1],
                s_rn)
            rn_prev = rn_tgt
            on_free[opar] = rn_tgt
        sc_after[h] = s_sc.n
        pv_after[h] = s_pv.n
    if pend_rc is not None:
        add("scalar", pend_rc)
        pend_rc = None
    if pend_rm is not None:
        add("tensor", pend_rm)
        pend_rm = None
    sc_all = s_sc.n
    pv_all = s_pv.n
    rn_all = s_rn.n

    p3_bank_free = [0] * 6
    p3_xr_free = [0, 0, 0, 0]
    x1b_free = [0, 0]
    w_tgts3 = {}
    w_tgts3[0] = dma("gpsimd", slab0[:], wo_p[0], s_wsl,
                     waits=[(s_sc, sc_all), (s_pv, pv_all)])
    p3_slab_free = [0, 0]
    for cc in range(NOC):
        par = cc % 2
        if cc + 1 < NOC:
            w_tgts3[cc + 1] = dma(
                "gpsimd", slabs[(cc + 1) % 2][:], wo_p[cc + 1], s_wsl,
                waits=[(s_p3m, p3_slab_free[(cc + 1) % 2]),
                       (s_pv, pv_all) if cc + 1 == 1 else (s_pv, 0)])
        for tt in range(4):
            dma("sync", xr_s[tt], xT_p[cc][:, tt * 512:(tt + 1) * 512],
                s_p3l, waits=[(s_x1, p3_xr_free[tt]),
                              (s_rn, rn_all) if (cc == 0 and tt == 0)
                              else (s_rn, 0)])
        xr_need = s_p3l.n

        bks3 = [(cc * 4 + b) % 6 for b in range(4)]

        def p3mm(t, par=par, w_tgt=w_tgts3[cc], bks3=tuple(bks3),
                 bf=tuple(p3_bank_free[bk] for bk in bks3)):
            t.wait_ge(s_wsl.h, w_tgt)
            t.wait_ge(s_rn.h, rn_all)
            for f in bf:
                if f:
                    t.wait_ge(s_x1.h, f)
            for jc in range(32):
                lhsT = slabs[par][:, jc * 128:(jc + 1) * 128]
                for b in range(4):
                    rhs = arena[:, jc * 2048 + b * 512:
                                jc * 2048 + b * 512 + 512]
                    ins = t.matmul(ps[bks3[b]][:], lhsT, rhs,
                                   start=(jc == 0), stop=(jc == 31))
            ins.then_inc(s_p3m.h, 1)
        add("tensor", p3mm)
        mm_tgt = s_p3m.bump(1)
        p3_slab_free[par] = mm_tgt

        for b in range(4):
            xp = b % 2
            e_tgt = step(
                "vector",
                [(s_p3m, mm_tgt), (s_p3l, xr_need), (s_x1st, x1b_free[xp])],
                lambda v, bk=bks3[b], b=b, xp=xp: v.tensor_add(
                    scr2[:, xp * 512:(xp + 1) * 512], ps[bk][:], xr_s[b]),
                s_x1)
            p3_bank_free[bks3[b]] = e_tgt
            p3_xr_free[b] = e_tgt
            x1b_free[xp] = dma(
                "sync", x1d[cc][:, b * 512:(b + 1) * 512],
                scr2[:, xp * 512:(xp + 1) * 512], s_x1st,
                waits=[(s_x1, e_tgt)])
    x1st_all = s_x1st.n
    p3m_all = s_p3m.n

    dma("gpsimd", BfT[:], Bf_p, s_bfl, waits=[(s_sc, sc_all), (s_pv, pv_all)])
    bfl_all = s_bfl.n

    obuf_free = [0, 0]
    bc5_prev = 0
    sf5_prev = 0
    out_prev_half = 0
    for th in range(2):
        ht0 = th * 1024
        AO = 32768 * (th % 2)
        BO = 32768 - AO
        a2w = [(s_p3m, p3m_all)] if th == 0 else [(s_mm4, s_mm4.n)]
        a2_tgt0 = dma("gpsimd", slab0[:], A2_p[0].rearrange("p c e -> p (c e)"),
                      s_a2l, waits=a2w)
        a2_tgt1 = dma("gpsimd", slab1[:], A2_p[1].rearrange("p c e -> p (c e)"),
                      s_a2l)
        a2_need = s_a2l.n

        for cc in range(NCC):
            rw = []
            if cc < 2:
                rw.append((s_x1st, x1st_all))
                if th == 1:
                    rw.append((s_mm4, s_mm4.n))
                    rw.append((s_x2, s_x2.n))
            rl_tgt = dma("sync", arena[:, AO + cc * 1024:AO + (cc + 1) * 1024],
                         x1d[cc][:, ht0:ht0 + 1024], s_rl4, waits=rw)
            sqw = [(s_rl4, rl_tgt), (s_sf4, s_sf4.n)]
            if cc == 0 and sf5_prev:
                sqw.append((s_sf5, sf5_prev))
            sq_tgt = step(
                "scalar", sqw,
                lambda sc, cc=cc, AO=AO: sc.activation(
                    scr1[:], arena[:, AO + cc * 1024:AO + (cc + 1) * 1024],
                    AF.Square),
                s_sq4)

            def stmm(t, cc=cc, rl_tgt=rl_tgt, sq_tgt=sq_tgt,
                     bc5_prev=bc5_prev, AO=AO):
                t.wait_ge(s_rl4.h, rl_tgt)
                t.wait_ge(s_sq4.h, sq_tgt)
                if cc == 0 and bc5_prev:
                    t.wait_ge(s_sf5.h, bc5_prev)
                for tt in range(2):
                    t.matmul(ps[4 + tt][0:1, :], onescol[:],
                             arena[:, AO + cc * 1024 + tt * 512:
                                   AO + cc * 1024 + (tt + 1) * 512],
                             start=(cc == 0), stop=(cc == NCC - 1))
                for tt in range(2):
                    bank = ps[6] if tt == 0 else ps[3]
                    ins = t.matmul(bank[0:1, :], onescol[:],
                                   scr1[:, tt * 512:(tt + 1) * 512],
                                   start=(cc == 0), stop=(cc == NCC - 1))
                ins.then_inc(s_sf4.h, 1)
            add("tensor", stmm)
            s_sf4.bump(1)
        sf_need = s_sf4.n

        def fin1(v, sf_need=sf_need, out_prev=out_prev_half):
            v.wait_ge(s_sf4.h, sf_need)
            if out_prev:
                v.wait_ge(s_out.h, out_prev)
            v.tensor_copy(strow[0][:, 0:512], ps[4][0:1, :])
            v.tensor_copy(strow[0][:, 512:1024], ps[5][0:1, :])
            v.tensor_copy(strow[1][:, 0:512], ps[6][0:1, :])
            v.tensor_copy(strow[1][:, 512:1024], ps[3][0:1, :])
            v.tensor_scalar_mul(strow[0], strow[0], 1.0 / C)
            v.tensor_scalar_mul(strow[1], strow[1], 1.0 / C)
            v.tensor_mul(strow[2], strow[0], strow[0])
            v.tensor_sub(strow[1], strow[1], strow[2]).then_inc(s_sf4.h, 1)
        add("vector", fin1)
        s_sf4.bump(1)

        def fin1b(sc, tgt=s_sf4.n):
            sc.wait_ge(s_sf4.h, tgt)
            sc.activation(strow[2], strow[1], AF.Ln, bias=LN_EPS)
            sc.activation(strow[3], strow[2], AF.Exp, scale=-0.5).then_inc(
                s_sf4.h, 1)
        add("scalar", fin1b)
        s_sf4.bump(1)

        def fin1c(v, tgt=s_sf4.n):
            v.wait_ge(s_sf4.h, tgt)
            v.tensor_mul(strow[0], strow[0], strow[3]).then_inc(s_sf4.h, 1)
        add("vector", fin1c)
        s_sf4.bump(1)

        for which in range(2):
            src = strow[3] if which == 0 else strow[0]
            lhsv = onesr1f if which == 0 else negr1f
            for tt in range(2):
                rm_t = step(
                    "tensor", [(s_sf4, s_sf4.n)],
                    lambda t, src=src, tt=tt, lhsv=lhsv: t.matmul(
                        ps[6][:], lhsv[:], src[:, tt * 512:(tt + 1) * 512],
                        start=True, stop=True),
                    s_sf4)
                step("vector", [(s_sf4, rm_t)],
                     lambda v, which=which, tt=tt: v.tensor_copy(
                         bcb[:, which * 1024 + tt * 512:
                             which * 1024 + (tt + 1) * 512], ps[6][:]),
                     s_sf4)
        bc_need = s_sf4.n

        nm_tgts = []
        for cc in range(NCC):
            nm_tgts.append(step("vector", [(s_sf4, bc_need)],
                 lambda v, cc=cc, AO=AO, BO=BO: (
                     v.tensor_mul(scr1[:],
                                  arena[:, AO + cc * 1024:
                                        AO + (cc + 1) * 1024],
                                  bcb[:, 0:1024]),
                     v.tensor_add(arena[:, BO + cc * 1024:
                                        BO + (cc + 1) * 1024],
                                  scr1[:], bcb[:, 1024:2048]))[-1],
                 s_nm4))
        nm_all = s_nm4.n

        for et in range(2):
            def hmm(t, et=et, nm_tgts=tuple(nm_tgts), a2_need=a2_need,
                    BO=BO):
                t.wait_ge(s_a2l.h, a2_need)
                for cc in range(NCC):
                    if et == 0:
                        t.wait_ge(s_nm4.h, nm_tgts[cc])
                    sb = slabs[cc // 16]
                    lhsT = sb[:, (cc % 16) * 256 + et * 128:
                              (cc % 16) * 256 + (et + 1) * 128]
                    for tt in range(2):
                        rhs = arena[:, BO + cc * 1024 + tt * 512:
                                    BO + cc * 1024 + (tt + 1) * 512]
                        ins = t.matmul(ps[et * 2 + tt][:], lhsT, rhs,
                                       start=(cc == 0), stop=(cc == NCC - 1))
                ins.then_inc(s_hm4.h, 1)
            add("tensor", hmm)
            hm_tgt = s_hm4.bump(1)
            for tt in range(2):
                step("vector", [(s_hm4, hm_tgt)],
                     lambda v, et=et, tt=tt: v.tensor_scalar_mul(
                         hT[:, et * 1024 + tt * 512:
                            et * 1024 + (tt + 1) * 512],
                         ps[et * 2 + tt][:], wppb[:, et:et + 1]),
                     s_he4)
        he_all = s_he4.n
        hm_all = s_hm4.n

        p4_bank_free = [0, 0, 0]
        pend_st2 = []
        w_tgts4 = {}
        w_tgts4[0] = dma("gpsimd", slab0[:], W_p[0], s_wsl,
                         waits=[(s_hm4, hm_all), (s_bfl, bfl_all)])
        p4_slab_free = [0, 0]
        for oc in range(NOC):
            par = oc % 2
            if oc + 1 < NOC:
                w_tgts4[oc + 1] = dma(
                    "gpsimd", slabs[(oc + 1) % 2][:], W_p[oc + 1], s_wsl,
                    waits=[(s_mm4, p4_slab_free[(oc + 1) % 2]),
                           (s_hm4, hm_all) if oc + 1 == 1 else (s_hm4, 0)])

            bk4 = [(oc * 2 + tt) % 3 for tt in range(2)]
            for f_ in pend_st2:
                add("tensor", f_)
            pend_st2 = []

            def p4mm(t, oc=oc, par=par, w_tgt=w_tgts4[oc], he_all=he_all,
                     bk4=tuple(bk4), bf=tuple(p4_bank_free[b] for b in bk4),
                     BO=BO):
                t.wait_ge(s_wsl.h, w_tgt)
                t.wait_ge(s_he4.h, he_all)
                for f in bf:
                    if f:
                        t.wait_ge(s_x2.h, f)
                for cc in range(NCC):
                    lhsT = slabs[par][:, cc * 128:(cc + 1) * 128]
                    for tt in range(2):
                        rhs = arena[:, BO + cc * 1024 + tt * 512:
                                    BO + cc * 1024 + (tt + 1) * 512]
                        t.matmul(ps[bk4[tt]][:], lhsT, rhs,
                                 start=(cc == 0), stop=False)
                for et in range(2):
                    lhsT = BfT[:, et * 4096 + oc * 128:
                               et * 4096 + (oc + 1) * 128]
                    for tt in range(2):
                        t.matmul(ps[bk4[tt]][:], lhsT,
                                 hT[:, et * 1024 + tt * 512:
                                    et * 1024 + (tt + 1) * 512],
                                 start=False, stop=False)
                for tt in range(2):
                    ins = t.matmul(ps[bk4[tt]][:],
                                   biasb[0:1, oc * 128:(oc + 1) * 128],
                                   onesrow[:], start=False, stop=True)
                ins.then_inc(s_mm4.h, 1)
            add("tensor", p4mm)
            mm_tgt = s_mm4.bump(1)
            p4_slab_free[par] = mm_tgt

            for tt in range(2):
                e_tgt = step(
                    "vector", [(s_mm4, mm_tgt)],
                    lambda v, oc=oc, tt=tt, bk=bk4[tt], AO=AO, BO=BO:
                    v.tensor_add(
                        arena[:, AO + oc * 1024 + tt * 512:
                              AO + oc * 1024 + (tt + 1) * 512],
                        ps[bk][:],
                        arena[:, BO + oc * 1024 + tt * 512:
                              BO + oc * 1024 + (tt + 1) * 512]),
                    s_x2)
                p4_bank_free[bk4[tt]] = e_tgt
                sq_tgt = step(
                    "scalar", [(s_x2, e_tgt), (s_sf5, s_sf5.n)],
                    lambda sc, oc=oc, tt=tt, AO=AO: sc.activation(
                        scr1[:, tt * 512:(tt + 1) * 512],
                        arena[:, AO + oc * 1024 + tt * 512:
                              AO + oc * 1024 + (tt + 1) * 512],
                        AF.Square),
                    s_sq5)

                def st2mm(t, oc=oc, tt=tt, e_tgt=e_tgt, sq_tgt=sq_tgt,
                          AO=AO):
                    t.wait_ge(s_x2.h, e_tgt)
                    t.wait_ge(s_sq5.h, sq_tgt)
                    t.matmul(ps[4 + tt][0:1, :], onescol[:],
                             arena[:, AO + oc * 1024 + tt * 512:
                                   AO + oc * 1024 + (tt + 1) * 512],
                             start=(oc == 0), stop=(oc == NOC - 1))
                    bank = ps[6] if tt == 0 else ps[3]
                    t.matmul(bank[0:1, :], onescol[:],
                             scr1[:, tt * 512:(tt + 1) * 512],
                             start=(oc == 0), stop=(oc == NOC - 1)
                             ).then_inc(s_sf5.h, 1)
                pend_st2.append(st2mm)
                s_sf5.bump(1)
        for f_ in pend_st2:
            add("tensor", f_)
        pend_st2 = []
        x2_prev_half = s_x2.n
        sf5_need = s_sf5.n

        def fin2(v, sf5_need=sf5_need):
            v.wait_ge(s_sf5.h, sf5_need)
            v.tensor_copy(strow[0][:, 0:512], ps[4][0:1, :])
            v.tensor_copy(strow[0][:, 512:1024], ps[5][0:1, :])
            v.tensor_copy(strow[1][:, 0:512], ps[6][0:1, :])
            v.tensor_copy(strow[1][:, 512:1024], ps[3][0:1, :])
            v.tensor_scalar_mul(strow[0], strow[0], 1.0 / C)
            v.tensor_scalar_mul(strow[1], strow[1], 1.0 / C)
            v.tensor_mul(strow[2], strow[0], strow[0])
            v.tensor_sub(strow[1], strow[1], strow[2]).then_inc(s_sf5.h, 1)
        add("vector", fin2)
        s_sf5.bump(1)

        def fin2b(sc, tgt=s_sf5.n):
            sc.wait_ge(s_sf5.h, tgt)
            sc.activation(strow[2], strow[1], AF.Ln, bias=LN_EPS)
            sc.activation(strow[3], strow[2], AF.Exp, scale=-0.5).then_inc(
                s_sf5.h, 1)
        add("scalar", fin2b)
        s_sf5.bump(1)

        def fin2c(v, tgt=s_sf5.n):
            v.wait_ge(s_sf5.h, tgt)
            v.tensor_mul(strow[0], strow[0], strow[3]).then_inc(s_sf5.h, 1)
        add("vector", fin2c)
        s_sf5.bump(1)

        for which in range(2):
            src = strow[3] if which == 0 else strow[0]
            lhsv = onesr1f if which == 0 else negr1f
            for tt in range(2):
                rm_t = step(
                    "tensor", [(s_sf5, s_sf5.n)],
                    lambda t, src=src, tt=tt, lhsv=lhsv: t.matmul(
                        ps[6][:], lhsv[:], src[:, tt * 512:(tt + 1) * 512],
                        start=True, stop=True),
                    s_sf5)
                step("vector", [(s_sf5, rm_t)],
                     lambda v, which=which, tt=tt: v.tensor_copy(
                         bcb[:, which * 1024 + tt * 512:
                             which * 1024 + (tt + 1) * 512], ps[6][:]),
                     s_sf5)
        bc5_need = s_sf5.n

        no5_tgts = []
        for oc in range(NOC):
            obp = oc % 2
            no_tgt = step(
                "vector",
                [(s_sf5, bc5_need), (s_out, obuf_free[obp])],
                lambda v, oc=oc, obp=obp, AO=AO: (
                    v.tensor_mul(ob_s[obp],
                                 arena[:, AO + oc * 1024:
                                       AO + (oc + 1) * 1024],
                                 bcb[:, 0:1024]),
                    v.tensor_add(ob_s[obp], ob_s[obp],
                                 bcb[:, 1024:2048]))[-1],
                s_no5)
            no5_tgts.append(no_tgt)
            obuf_free[obp] = dma(
                "sync", out_d[oc * 128:(oc + 1) * 128, ht0:ht0 + 1024],
                ob_s[obp], s_out, waits=[(s_no5, no_tgt)])
        bc5_prev = bc5_need
        sf5_prev = sf5_need
        out_prev_half = s_out.n

    out_all = s_out.n

    def fin(s):
        s.wait_ge(s_out.h, out_all)
    add("sync", fin)

    with nc.Block() as block:
        @block.sync
        def _(en):
            rt["sync_role"] = nc.partition_id(engines=[ET.SP]) % 2
            for fn in ops["sync"]:
                fn(en)

        @block.gpsimd
        def _(en):
            pid = nc.partition_id(engines=[ET.Pool])
            rt["gp_opp"] = (pid + 1) % 2
            for fn in ops["gpsimd"]:
                fn(en)

        @block.tensor
        def _(en):
            for fn in ops["tensor"]:
                fn(en)

        @block.vector
        def _(en):
            for fn in ops["vector"]:
                fn(en)

        @block.scalar
        def _(en):
            for fn in ops["scalar"]:
                fn(en)

    ctx.close()
    return nc


_NC_CACHE = [None]


def _get_nc():
    if _NC_CACHE[0] is None:
        _NC_CACHE[0] = _build_nc()
    return _NC_CACHE[0]


def _host_inputs(x, wq, wk, wv, wo, W, bias, A, Bf, coeffs, scales):
    x = np.asarray(x)

    def perm4(w):
        return np.ascontiguousarray(
            np.asarray(w).T.reshape(NCC, 128, NOC, 128).transpose(2, 1, 0, 3))

    wq_p, wk_p, wv_p = perm4(wq), perm4(wk), perm4(wv)
    wo_p, W_p = perm4(wo), perm4(W)
    A2 = np.asarray(A).reshape(256, C)
    A2T = np.ascontiguousarray(A2.T)
    A2_p = np.ascontiguousarray(
        A2T.reshape(2, 16, 128, 256).transpose(0, 2, 1, 3))
    Bf_re = np.ascontiguousarray(
        np.asarray(Bf).transpose(0, 2, 1).reshape(256, C))
    Bf_p = np.ascontiguousarray(Bf_re.reshape(2, 128, C).transpose(1, 0, 2))
    bias_in = np.ascontiguousarray(np.asarray(bias).reshape(1, C))
    coef_in = np.ascontiguousarray(
        np.asarray(coeffs).reshape(16, 1).astype(np.float32))
    scal_in = np.ascontiguousarray(
        np.asarray(scales).reshape(16, 1).astype(np.float32))
    m16 = np.zeros((16, 256), dtype=np.float32)
    for ee in range(16):
        m16[ee, ee * 16:(ee + 1) * 16] = 1.0
    ident = np.eye(128, dtype=np.float32).astype(bfdt)
    nonce = np.array([[np.random.randint(1, 2**31 - 1)]], dtype=np.uint32)

    maps = []
    for pid in range(8):
        b, r = pid // 2, pid % 2
        own_blocks = np.arange(r, 32, 2)
        own_tok = (own_blocks[:, None] * 128 + np.arange(128)[None, :]).ravel()
        xb = np.ascontiguousarray(x[b][own_tok, :])
        xT = np.ascontiguousarray(xb.T.reshape(NCC, 128, TOK))
        gpos = own_tok.astype(np.float32)
        cosr = np.ascontiguousarray(
            np.broadcast_to(np.cos(gpos)[None, :], (128, TOK))
        ).astype(np.float32)
        sgn = np.where(np.arange(128) % 2 == 0, -1.0, 1.0)[:, None]
        salr = (sgn * np.sin(gpos)[None, :]).astype(np.float32)
        mask = np.zeros((128, 8, 512), dtype=np.float32)
        qq = np.arange(512)
        qrel = 2 * (qq // 128) * 128 + r * 128 + (qq % 128)
        for d in range(8):
            kk = np.arange(128)[:, None]
            mask[:, d, :] = np.where((d * 128 + kk) > qrel[None, :],
                                     -1e30, 0.0)
        maps.append({
            "xT_p": xT.astype(np.float32),
            "wq_p": wq_p, "wk_p": wk_p, "wv_p": wv_p, "wo_p": wo_p,
            "W_p": W_p, "A2_p": A2_p, "Bf_p": Bf_p,
            "bias_in": bias_in.astype(np.float32),
            "coef_in": coef_in, "scal_in": scal_in, "m16_in": m16,
            "cos_in": cosr, "sal_in": salr,
            "mask_in": np.ascontiguousarray(
                mask.reshape(128, 4096)).astype(bfdt),
            "ident_in": ident,
            "nonce_in": nonce,
        })
    return maps


def kernel(L0_attn_x, L0_attn_wq, L0_attn_wk, L0_attn_wv, L0_attn_wo,
           L0_ffn_W, L0_ffn_bias, L0_ffn_A, L0_ffn_Bf, L0_ffn_coeffs,
           L0_ffn_scales):
    nc = _get_nc()
    in_maps = _host_inputs(L0_attn_x, L0_attn_wq, L0_attn_wk, L0_attn_wv,
                           L0_attn_wo, L0_ffn_W, L0_ffn_bias, L0_ffn_A,
                           L0_ffn_Bf, L0_ffn_coeffs, L0_ffn_scales)
    res = run_bass_kernel_spmd(nc, in_maps, core_ids=list(range(8)))
    out = np.zeros((B, T, C), dtype=np.float32)
    for pid in range(8):
        b, r = pid // 2, pid % 2
        own_blocks = np.arange(r, 32, 2)
        own_tok = (own_blocks[:, None] * 128 + np.arange(128)[None, :]).ravel()
        out[b][own_tok, :] = res.results[pid]["out_fm"].T
    return out


# version 18
# speedup vs baseline: 1.0704x; 1.0704x over previous
import math
import numpy as np
import ml_dtypes

import concourse.bass as bass
import concourse.mybir as mybir
from concourse.bass_utils import run_bass_kernel_spmd

F32 = mybir.dt.float32
BF16 = mybir.dt.bfloat16
U32 = mybir.dt.uint32
AF = mybir.ActivationFunctionType
ALU = mybir.AluOpType
ET = mybir.EngineType

B, T, C, H = 4, 4096, 4096, 32
TOK = 2048
NCC = 32
NOC = 32
SCALE = 1.0 / math.sqrt(128.0)
EXP_BIAS = -24.0
LN_EPS = 1e-5
bfdt = ml_dtypes.bfloat16


class Sem:
    def __init__(self, nc, name):
        self.h = nc.alloc_semaphore(name)
        self.n = 0

    def bump(self, k):
        self.n += k
        return self.n


def _build_nc():
    nc = bass.Bass("TRN2", target_bir_lowering=False,
                   detect_race_conditions=False)

    def din(name, shape, dt=F32):
        return nc.dram_tensor(name, list(shape), dt, kind="ExternalInput").ap()

    xT_p = din("xT_p", [NCC, 128, TOK])
    wq_p = din("wq_p", [NOC, 128, NCC, 128])
    wk_p = din("wk_p", [NOC, 128, NCC, 128])
    wv_p = din("wv_p", [NOC, 128, NCC, 128])
    wo_p = din("wo_p", [NOC, 128, NCC, 128])
    W_p = din("W_p", [NOC, 128, NCC, 128])
    A2_p = din("A2_p", [2, 128, 16, 256])
    Bf_p = din("Bf_p", [128, 2, C])
    bias_in = din("bias_in", [1, C])
    coef_in = din("coef_in", [16, 1])
    scal_in = din("scal_in", [16, 1])
    m16_in = din("m16_in", [16, 256])
    cos_in = din("cos_in", [128, TOK])
    sal_in = din("sal_in", [128, TOK])
    mask_in = din("mask_in", [128, 4096], BF16)
    ident_in = din("ident_in", [128, 128], BF16)
    nonce_in = din("nonce_in", [1, 1], U32)

    out_d = nc.dram_tensor("out_fm", [C, TOK], F32, kind="ExternalOutput").ap()

    qtd = nc.dram_tensor("qtd", [C, TOK], BF16, kind="Internal").ap()
    x1d = nc.dram_tensor("x1d", [NCC, 128, TOK], BF16, kind="Internal").ap()
    kloc = nc.dram_tensor("kloc", [C, TOK], BF16, kind="Internal").ap()
    vloc = nc.dram_tensor("vloc", [H, 128, 16, 128], BF16,
                          kind="Internal").ap()
    shK = nc.dram_tensor("shK", [2, C, TOK], BF16, kind="Internal",
                         addr_space="Shared").ap()
    shV = nc.dram_tensor("shV", [H, 128, 32, 128], BF16, kind="Internal",
                         addr_space="Shared").ap()
    shF = nc.dram_tensor("shF", [2, 16], U32, kind="Internal",
                         addr_space="Shared").ap()

    from contextlib import ExitStack
    ctx = ExitStack()
    e = ctx.enter_context

    arena = e(nc.sbuf_tensor("arena", [128, 65536], BF16))
    slab0 = e(nc.sbuf_tensor("slab0", [128, 4096], BF16))
    slab1 = e(nc.sbuf_tensor("slab1", [128, 4096], BF16))
    slabs = [slab0, slab1]
    poolb = e(nc.sbuf_tensor("poolb", [128, 5120], BF16))
    poolc = e(nc.sbuf_tensor("poolc", [128, 8192], BF16))
    poolf = e(nc.sbuf_tensor("poolf", [128, 2048], F32))
    scr1 = e(nc.sbuf_tensor("scr1", [128, 1024], BF16))
    scr2 = e(nc.sbuf_tensor("scr2", [128, 1024], BF16))
    scr3 = e(nc.sbuf_tensor("scr3", [128, 1024], BF16))
    vtb = e(nc.sbuf_tensor("vtb", [128, 2048], BF16))
    biasb = e(nc.sbuf_tensor("biasb", [1, 4096], BF16))
    bcb = e(nc.sbuf_tensor("bcb", [128, 2048], F32))
    onesrow = e(nc.sbuf_tensor("onesrow", [1, 512], BF16))
    pmr1f = e(nc.sbuf_tensor("pmr1f", [1, 256], F32))
    onesr1f = pmr1f[:, 0:128]
    negr1f = pmr1f[:, 128:256]
    onescol = e(nc.sbuf_tensor("onescol", [128, 1], BF16))
    onescolf = e(nc.sbuf_tensor("onescolf", [128, 1], F32))
    m16x = e(nc.sbuf_tensor("m16x", [16, 258], BF16))
    m16b = m16x[:, 0:256]
    w16c = m16x[:, 256:257]
    idb = e(nc.sbuf_tensor("idb", [128, 128], BF16))
    wppb4 = e(nc.sbuf_tensor("wppb4", [128, 2], F32))
    wppb = wppb4[:, 0:2]
    cf32 = wppb4[0:16, 0:2]
    cstf = e(nc.sbuf_tensor("cstf", [128, 4], F32))
    nc.const_aps.register(F32, EXP_BIAS, cstf[:, 0:1])
    nc.const_aps.register(F32, LN_EPS, cstf[:, 1:2])
    nc.const_aps.register(F32, SCALE, cstf[:, 2:3])
    nc.const_aps.register(F32, -0.5, cstf[:, 3:4])
    nc.const_aps.register(F32, -1.0, negr1f[:, 0:1])
    nfb = e(nc.sbuf_tensor("nfbuf", [1, 8], U32))
    nb = nfb[:, 0:1]
    fb = nfb[:, 4:5]

    qc_s = [poolf[:, 0:512], poolf[:, 512:1024]]
    qs_s = [poolf[:, 1024:1536], poolf[:, 1536:2048]]
    Rs = poolf[:, 0:512]
    rl = poolf[0:1, 512:1024]
    rlg = poolf[0:1, 0:512]
    xr_s = [poolf[:, i * 512:(i + 1) * 512] for i in range(4)]
    ob_s = [poolf[:, 0:1024], poolf[:, 1024:2048]]
    strow = [poolf[0:1, 0:1024], poolf[32:33, 0:1024],
             poolf[32:33, 1024:2048], poolf[0:1, 1024:2048]]
    qts = [poolb[:, 0:2048], poolb[:, 2048:4096]]
    ptil = [poolb[:, 4096:4608], poolb[:, 4608:5120], scr3[:, 0:512]]
    cosb = poolc[:, 0:2048]
    salb = poolc[:, 2048:4096]
    vbufB = poolc[:, 0:4096]
    maskb = poolc[:, 4096:8192]
    BfT = poolc
    hT = vtb
    lacc = [bcb[:, 0:512], bcb[:, 512:1024]]
    laccb = poolf[:, 1024:1280].bitcast(BF16)

    ps = [e(nc.psum_tensor(f"ps{i}", [128, 512], F32)) for i in range(7)]
    psT = e(nc.psum_tensor("psT", [128, 512], BF16))
    psT2 = ps[6][:, 0:256].bitcast(BF16)

    S = lambda name: Sem(nc, name)
    s_cst = S("cst"); s_csh = S("csh"); s_vst = S("vst"); s_wpp = S("wpp")
    s_xld = S("xld"); s_wsl = S("wsl")
    s_mm1 = S("mm1"); s_act1 = S("act1")
    s_dve1 = S("dve1"); s_pl1 = S("pl1"); s_st1 = S("st1")
    s_vtr = S("vtr"); s_vev = S("vev"); s_st2 = S("st2")
    s_fl = S("fl"); s_x = S("x"); s_lc = S("lc")
    s_kld = S("kld"); s_sc = S("sc"); s_ex = S("ex")
    s_pv = S("pv"); s_rc = S("rc"); s_rm = S("rm"); s_rn = S("rn")
    s_lv = S("lv"); s_ls = S("ls")
    s_p3l = S("p3l"); s_p3m = S("p3m"); s_x1 = S("x1"); s_x1st = S("x1st")
    s_a2l = S("a2l"); s_bfl = S("bfl")
    s_rl4 = S("rl4"); s_sq4 = S("sq4"); s_sf4 = S("sf4"); s_nm4 = S("nm4")
    s_hm4 = S("hm4"); s_he4 = S("he4"); s_mm4 = S("mm4"); s_x2 = S("x2")
    s_sq5 = S("sq5"); s_sf5 = S("sf5"); s_no5 = S("no5"); s_out = S("out")

    ops = {k: [] for k in ("sync", "gpsimd", "tensor", "vector", "scalar")}
    rt = {}

    def add(eng, fn):
        ops[eng].append(fn)

    def dma(eng, out, in_, sem, waits=(), n=16):
        wl = list(waits)
        if eng == "gpsimd" and sem.n:
            wl.append((sem, sem.n))
        tgt = sem.bump(n)

        def fn(en, out=out, in_=in_, wl=wl, sem=sem, n=n):
            for s, v in wl:
                if v:
                    en.wait_ge(s.h, v)
            en.dma_start(out=out, in_=in_).then_inc(sem.h, n)
        add(eng, fn)
        return tgt

    def step(eng, waits, emit, sem=None, n=1, sem2=None):
        tgt = sem.bump(n) if sem is not None else None
        if sem2 is not None:
            sem2.bump(1)
        wl = list(waits)

        def fn(en, wl=wl, emit=emit, sem=sem, n=n, sem2=sem2):
            for s, v in wl:
                if v:
                    en.wait_ge(s.h, v)
            ins = emit(en)
            if sem is not None:
                ins = ins.then_inc(sem.h, n)
            if sem2 is not None:
                en.nop().then_inc(sem2.h, 1)
        add(eng, fn)
        return tgt

    dma("gpsimd", biasb[:], bias_in, s_cst)
    dma("gpsimd", cosb, cos_in, s_cst)
    dma("gpsimd", salb, sal_in, s_cst)
    dma("gpsimd", m16b[:], m16_in, s_cst)
    dma("sync", idb[:], ident_in, s_csh)
    dma("sync", maskb, mask_in, s_csh)
    dma("sync", nb[:], nonce_in, s_csh)
    dma("gpsimd", cf32[:, 0:1], coef_in, s_cst)
    dma("gpsimd", cf32[:, 1:2], scal_in, s_cst)
    cst_all = s_cst.n
    csh_all = s_csh.n

    def consts_v(v):
        v.wait_ge(s_cst.h, cst_all)
        v.wait_ge(s_csh.h, csh_all)
        v.memset(onescol[:], 1.0)
        v.memset(onescolf[:], 1.0)
        v.memset(cstf[:, 0:1], EXP_BIAS)
        v.memset(cstf[:, 1:2], LN_EPS)
        v.memset(cstf[:, 2:3], SCALE)
        v.memset(cstf[:, 3:4], -0.5)
        v.memset(onesrow[:], 1.0)
        v.memset(onesr1f[:], 1.0)
        v.memset(negr1f[:], -1.0)
        v.tensor_mul(w16c[:], cf32[:, 0:1], cf32[:, 1:2]).then_inc(s_vst.h, 1)
    add("vector", consts_v)
    s_vst.bump(1)

    for et in range(2):
        step("tensor", [(s_vst, 1), (s_wpp, s_wpp.n)],
             lambda t, et=et: t.matmul(ps[6][:, 0:1],
                                       m16b[:, et * 128:(et + 1) * 128],
                                       w16c[:], start=True, stop=True),
             s_wpp)
        step("vector", [(s_wpp, s_wpp.n)],
             lambda v, et=et: v.tensor_copy(wppb[:, et:et + 1], ps[6][:, 0:1]),
             s_wpp)
    wpp_all = s_wpp.n

    dma("gpsimd", arena[:].rearrange("p (c t) -> p c t", t=TOK),
        xT_p.rearrange("c p t -> p c t"), s_xld)
    xld_all = s_xld.n

    wsrc = {"q": wq_p, "k": wk_p, "v": wv_p}
    kinds = [("q", oc) for oc in range(NOC)] + \
            [("k", oc) for oc in range(NOC)] + \
            [("v", oc) for oc in range(NOC)]

    slab_free = [0, 0]
    bank_free = [0] * 6
    r_mul1 = [0, 0]
    r_pool = [0, 0]
    r_store = [0, 0]
    v_store = 0
    vev_bank = [0, 0]
    gtile = 0

    w_tgts = {}
    kind0, oc0 = kinds[0]
    w_tgts[0] = dma("gpsimd", slabs[0][:], wsrc[kind0][oc0], s_wsl,
                    waits=[(s_cst, cst_all)])

    pend_vtrs = []
    pend_vvs = []
    last_vvf_sl = {0: 0, 1: 0}
    st2_by_oc = {}
    v_store = 0
    for ki, (kind, oc) in enumerate(kinds):
        par = ki % 2
        if kind == "v" and oc == 0:
            st1_qk = s_st1.n
            dve1_v0 = s_dve1.n

            def pubk(s, st1_qk=st1_qk):
                s.wait_ge(s_st1.h, st1_qk)
                role = rt["sync_role"]
                for i in range(8):
                    s.dma_start(
                        out=shK[bass.ts(role, 1), i * 512:(i + 1) * 512, :],
                        in_=kloc[i * 512:(i + 1) * 512, :]).then_inc(
                        s_fl.h, 16)
            add("sync", pubk)
            s_fl.bump(128)
        if ki + 1 < len(kinds):
            k2, o2 = kinds[ki + 1]
            w_tgts[ki + 1] = dma("gpsimd", slabs[(ki + 1) % 2][:],
                                 wsrc[k2][o2], s_wsl,
                                 waits=[(s_mm1, slab_free[(ki + 1) % 2])])

        bks = [(ki * 4 + b) % 6 for b in range(4)]

        def mmblk(t, par=par, w_tgt=w_tgts[ki], bks=tuple(bks),
                  bf=tuple(bank_free[bk] for bk in bks)):
            t.wait_ge(s_wsl.h, w_tgt)
            t.wait_ge(s_xld.h, xld_all)
            for b in range(4):
                if bf[b]:
                    t.wait_ge(s_act1.h, bf[b])
                for c in range(NCC):
                    lhsT = slabs[par][:, c * 128:(c + 1) * 128]
                    rhs = arena[:, c * 2048 + b * 512: c * 2048 + b * 512 + 512]
                    ins = t.matmul(ps[bks[b]][:], lhsT, rhs,
                                   start=(c == 0), stop=(c == NCC - 1))
                ins.then_inc(s_mm1.h, 1)
        add("tensor", mmblk)
        mm_base = s_mm1.n
        mm_tgt = s_mm1.bump(4)
        slab_free[par] = mm_tgt
        for f_ in pend_vtrs:
            add("tensor", f_)
        pend_vtrs = []
        vv_flush = pend_vvs
        pend_vvs = []

        for f_ in vv_flush:
            add("scalar", f_)
        vv_flush = []
        if kind in ("q", "k"):
            for b in range(4):
                tp = gtile % 2
                act_tgt = step(
                    "scalar",
                    [(s_mm1, mm_base + b + 1), (s_dve1, r_mul1[tp])],
                    lambda sc, bk=bks[b], tp=tp: sc.copy(qc_s[tp], ps[bk][:]),
                    s_act1)
                bank_free[bks[b]] = act_tgt

                shmask = [i ^ 1 for i in range(32)]
                sh_tgt = step(
                    "vector", [(s_act1, act_tgt)],
                    lambda v, tp=tp, shmask=shmask: v.stream_shuffle(
                        qs_s[tp], qc_s[tp], shmask),
                    s_dve1)
                m1_tgt = step(
                    "vector", [(s_pl1, r_pool[tp])],
                    lambda v, b=b, tp=tp: v.tensor_mul(
                        scr1[:, tp * 512:(tp + 1) * 512], qc_s[tp],
                        cosb[:, b * 512:(b + 1) * 512]),
                    s_dve1)
                r_mul1[tp] = m1_tgt
                p2_tgt = step(
                    "gpsimd", [(s_dve1, sh_tgt)],
                    lambda g, b=b, tp=tp: g.tensor_mul(
                        scr2[:, tp * 512:(tp + 1) * 512], qs_s[tp],
                        salb[:, b * 512:(b + 1) * 512]),
                    s_pl1)
                r_pool[tp] = p2_tgt
                ad_tgt = step(
                    "vector",
                    [(s_pl1, p2_tgt), (s_st1, r_store[tp])],
                    lambda v, tp=tp: v.tensor_add(
                        scr3[:, tp * 512:(tp + 1) * 512],
                        scr1[:, tp * 512:(tp + 1) * 512],
                        scr2[:, tp * 512:(tp + 1) * 512]),
                    s_dve1)
                if kind == "q":
                    st_tgt = dma(
                        "sync",
                        qtd[oc * 128:(oc + 1) * 128, b * 512:(b + 1) * 512],
                        scr3[:, tp * 512:(tp + 1) * 512], s_st1,
                        waits=[(s_dve1, ad_tgt)])
                else:
                    st_tgt = dma(
                        "sync",
                        kloc[oc * 128:(oc + 1) * 128, b * 512:(b + 1) * 512],
                        scr3[:, tp * 512:(tp + 1) * 512], s_st1,
                        waits=[(s_dve1, ad_tgt)])
                r_store[tp] = st_tgt
                gtile += 1
        else:
            vtr_base = s_vtr.n
            st_prev = v_store
            for b in range(4):
                evs = [scr1[:, 0:512], scr1[:, 512:1024],
                       scr2[:, 0:512], scr2[:, 512:1024]][b]
                evw = [(s_mm1, mm_base + b + 1), (s_vtr, vtr_base)]
                if vtr_base == 0:
                    evw.append((s_dve1, dve1_v0))
                ev_tgt = step(
                    "scalar", evw,
                    lambda sc, bk=bks[b], evs=evs: sc.copy(evs, ps[bk][:]),
                    s_act1)
                bank_free[bks[b]] = ev_tgt

                pst = psT if b % 2 == 0 else psT2

                def vtr_blk(t, evs=evs, ev_tgt=ev_tgt, pst=pst,
                            vev_need=last_vvf_sl[b % 2]):
                    t.wait_ge(s_act1.h, ev_tgt)
                    if vev_need:
                        t.wait_ge(s_vev.h, vev_need)
                    for sub in range(4):
                        ins = t.transpose(
                            pst[:, sub * 128:(sub + 1) * 128],
                            evs[:, sub * 128:(sub + 1) * 128],
                            idb[:])
                    ins.then_inc(s_vtr.h, 1)
                pend_vtrs.append(vtr_blk)
                vtr_tgt = s_vtr.bump(1)

                def vvf(sc, b=b, pst=pst, vtr_tgt=vtr_tgt,
                        st_prev=st_prev):
                    sc.wait_ge(s_vtr.h, vtr_tgt)
                    if st_prev:
                        sc.wait_ge(s_st2.h, st_prev)
                    sc.copy(vtb[:, b * 512:(b + 1) * 512],
                            pst[:]).then_inc(s_vev.h, 1)
                pend_vvs.append(vvf)
                vv_tgt = s_vev.bump(1)
                last_vvf_sl[b % 2] = vv_tgt

            v_store = dma("sync", vloc[oc], vtb[:, 0:2048], s_st2,
                          waits=[(s_vev, vv_tgt)])
            st2_by_oc[oc] = v_store
            if oc % 4 == 3:
                def pubvc(s, tgt=v_store, hc=oc // 4):
                    s.wait_ge(s_st2.h, tgt)
                    role = rt["sync_role"]
                    s.dma_start(
                        out=shV[hc * 4:(hc + 1) * 4, :,
                                bass.ts(role, 16), :],
                        in_=vloc[hc * 4:(hc + 1) * 4]).then_inc(s_fl.h, 16)
                add("sync", pubvc)
                s_fl.bump(16)

    for f_ in pend_vtrs:
        add("tensor", f_)
    pend_vtrs = []
    for f_ in pend_vvs:
        add("scalar", f_)
    pend_vvs = []
    st1_all = s_st1.n
    st2_all = s_st2.n
    mm1_all = s_mm1.n

    fl_pub = s_fl.n

    def flagw(s, fl_pub=fl_pub):
        s.wait_ge(s_fl.h, fl_pub)
        role = rt["sync_role"]
        s.dma_start(out=shF[bass.ts(role, 1), 0:1], in_=nb[:]).then_inc(
            s_fl.h, 16)
    add("sync", flagw)
    fl_all = s_fl.bump(16)

    def poll(g, fl_all=fl_all):
        g.wait_ge(s_fl.h, fl_all)
        opp = rt["gp_opp"]
        ms = nc.monotonic_semaphore(0, engine=g)
        with (g.register("rn_") as rn_, g.register("rf_") as rf_,
              g.register("rc_") as rc_):
            g.load(rn_, nb[0:1, 0:1])
            g.reg_mov(rc_, 1)
            with g.While(rc_):
                g.dma_start(out=fb[:], in_=shF[bass.ts(opp, 1), 0:1]
                            ).then_inc(ms.sem(), 16)
                ms.wait_inc(16)
                g.load(rf_, fb[0:1, 0:1])
                g.reg_alu(rc_, rf_, rn_, ALU.not_equal)
        g.sem_inc(s_x.h, 1)
    add("gpsimd", poll)
    s_x.bump(1)

    sc_after = [0] * (H + 1)
    pv_after = [0] * (H + 1)
    exp_free = [0, 0, 0]
    sbank = [ps[0], ps[1], ps[5]]
    pv_free = [0, 0, 0]
    lv_free = [0, 0, 0]
    ls_free = [0, 0]
    on_free = [0, 0]
    rm_prev = 0
    rn_prev = 0
    pend_pv = []
    pend_rm = None
    pend_rc = None
    rc_prev = 0
    lc_prev = 0

    for h in range(H):
        hb = h % 2
        vb = slab1[:] if hb == 0 else vbufB
        vw = [(s_x, 1), (s_csh, csh_all if h == 0 else 0)]
        if h >= 2:
            vw.append((s_pv, pv_after[h - 2]))
        if h == 1:
            vw.append((s_pl1, s_pl1.n))
        dma("sync", vb, shV[h], s_kld, waits=vw)
        qw = [(s_sc, sc_after[h - 2])] if h >= 2 else []
        dma("sync", qts[hb], qtd[h * 128:(h + 1) * 128, :], s_kld, waits=qw)
        kw = [(s_sc, sc_after[h - 1])] if h >= 1 else []
        for rr in range(2):
            for kc in range(4):
                dma("sync",
                    slab0[:, rr * 2048 + kc * 512:rr * 2048 + (kc + 1) * 512],
                    shK[rr, h * 128:(h + 1) * 128, kc * 512:(kc + 1) * 512],
                    s_kld, waits=kw)
                kw = []
        kld_need = s_kld.n

        for qt in range(4):
            nk = 8 * qt + 8
            opar = (h * 4 + qt) % 2
            bo = ps[2 + opar]
            bl = ps[4]
            for j in range(nk):
                ab = (s_sc.n) % 3
                pb = (s_sc.n) % 3

                def scmm(t, hb=hb, qt=qt, j=j, ab=ab, kld_need=kld_need,
                         exf=exp_free[ab],
                         onf=(on_free[opar] if j == 0 else 0)):
                    t.wait_ge(s_kld.h, kld_need)
                    if exf:
                        t.wait_ge(s_ex.h, exf)
                    if onf:
                        t.wait_ge(s_rn.h, onf)
                    col = (j % 2) * 2048 + (j // 2) * 128
                    d = j - 8 * qt
                    ba = sbank[ab]
                    lhsT = slab0[:, col:col + 128]
                    rhs = qts[hb][:, qt * 512:(qt + 1) * 512]
                    if d < 0:
                        t.matmul(ba[:], lhsT, rhs, start=True, stop=True
                                 ).then_inc(s_sc.h, 1)
                    else:
                        t.matmul(ba[:], lhsT, rhs, start=True, stop=False)
                        t.matmul(ba[:], idb[:],
                                 maskb[:, d * 512:(d + 1) * 512],
                                 start=False, stop=True).then_inc(s_sc.h, 1)
                add("tensor", scmm)
                sc_tgt = s_sc.bump(1)
                if len(pend_pv) >= 3:
                    add("tensor", pend_pv.pop(0))
                if j == 2 and pend_rc is not None:
                    add("scalar", pend_rc)
                    pend_rc = None
                if j == 3 and pend_rm is not None:
                    add("tensor", pend_rm)
                    pend_rm = None

                ex_tgt = step(
                    "scalar",
                    [(s_sc, sc_tgt), (s_pv, pv_free[pb]),
                     (s_lv, lv_free[pb])],
                    lambda sc, ab=ab, pb=pb: sc.activation(
                        ptil[pb], sbank[ab][:], AF.Exp,
                        bias=EXP_BIAS, scale=SCALE),
                    s_ex)
                exp_free[ab] = ex_tgt

                def pvmm(t, j=j, pb=pb, ex_tgt=ex_tgt, bo=bo,
                         vb=vb, first=(j == 0), last=(j == nk - 1)):
                    t.wait_ge(s_ex.h, ex_tgt)
                    sl = (j % 2) * 16 + (j // 2)
                    t.matmul(bo[:], vb[:, sl * 128:(sl + 1) * 128],
                             ptil[pb], start=first, stop=last
                             ).then_inc(s_pv.h, 1)
                pend_pv.append(pvmm)
                pv_free[pb] = s_pv.bump(1)
                lw = [(s_ex, ex_tgt)]
                if j == 0:
                    lw.append((s_ls, ls_free[opar]))
                if j == 0:
                    lv_free[pb] = step(
                        "vector", lw,
                        lambda v, pb=pb, opar=opar: v.tensor_copy(
                            lacc[opar], ptil[pb]),
                        s_lv)
                else:
                    lv_free[pb] = step(
                        "vector", lw,
                        lambda v, pb=pb, opar=opar: v.tensor_add(
                            lacc[opar], lacc[opar], ptil[pb]),
                        s_lv)

            while pend_pv:
                add("tensor", pend_pv.pop(0))
            lv_need = s_lv.n

            lcw = [(s_lv, lv_need), (s_ls, lc_prev)]
            if lc_prev == 0:
                lcw.append((s_pl1, s_pl1.n))
            lc_tgt = step(
                "vector", lcw,
                lambda v, opar=opar: v.tensor_copy(laccb, lacc[opar]),
                s_lc)

            def lsum(t, lc_tgt=lc_tgt, bl=bl,
                     rc_prev=rc_prev):
                t.wait_ge(s_lc.h, lc_tgt)
                if rc_prev:
                    t.wait_ge(s_rc.h, rc_prev)
                t.matmul(bl[0:1, :], onescol[:], laccb,
                         start=True, stop=True).then_inc(s_ls.h, 1)
            add("tensor", lsum)
            ls_tgt = s_ls.bump(1)
            ls_free[opar] = ls_tgt
            lc_prev = ls_tgt

            def rcf(sc, ls_tgt=ls_tgt, rm_prev=rm_prev, rn_prev=rn_prev,
                    bl=bl):
                sc.wait_ge(s_ls.h, ls_tgt)
                if rm_prev:
                    sc.wait_ge(s_rm.h, rm_prev)
                if rn_prev:
                    sc.wait_ge(s_rn.h, rn_prev)
                sc.activation(rlg, bl[0:1, :], AF.Ln)
                sc.activation(rl, rlg, AF.Exp, scale=-1.0).then_inc(
                    s_rc.h, 1)
            pend_rc = rcf
            rc_tgt = s_rc.bump(1)
            rc_prev = rc_tgt

            def rmmm(t, rc_tgt=rc_tgt, rn_prev=rn_prev):
                t.wait_ge(s_rc.h, rc_tgt)
                t.wait_ge(s_wpp.h, wpp_all)
                if rn_prev:
                    t.wait_ge(s_rn.h, rn_prev)
                t.matmul(ps[6][:], onesr1f[:], rl,
                         start=True, stop=True).then_inc(s_rm.h, 1)
            pend_rm = rmmm
            rm_tgt = s_rm.bump(1)
            rm_prev = rm_tgt
            rn_tgt = step(
                "vector", [(s_rm, rm_tgt)],
                lambda v, h=h, qt=qt, bo=bo: (
                    v.tensor_copy(Rs, ps[6][:]),
                    v.tensor_mul(
                        arena[:, h * 2048 + qt * 512:
                              h * 2048 + (qt + 1) * 512],
                        bo[:], Rs))[-1],
                s_rn)
            rn_prev = rn_tgt
            on_free[opar] = rn_tgt
        sc_after[h] = s_sc.n
        pv_after[h] = s_pv.n
    if pend_rc is not None:
        add("scalar", pend_rc)
        pend_rc = None
    if pend_rm is not None:
        add("tensor", pend_rm)
        pend_rm = None
    sc_all = s_sc.n
    pv_all = s_pv.n
    rn_all = s_rn.n

    p3_bank_free = [0] * 6
    p3_xr_free = [0, 0, 0, 0]
    x1b_free = [0, 0]
    w_tgts3 = {}
    w_tgts3[0] = dma("gpsimd", slab0[:], wo_p[0], s_wsl,
                     waits=[(s_sc, sc_all), (s_pv, pv_all)])
    p3_slab_free = [0, 0]
    for cc in range(NOC):
        par = cc % 2
        if cc + 1 < NOC:
            w_tgts3[cc + 1] = dma(
                "gpsimd", slabs[(cc + 1) % 2][:], wo_p[cc + 1], s_wsl,
                waits=[(s_p3m, p3_slab_free[(cc + 1) % 2]),
                       (s_pv, pv_all) if cc + 1 == 1 else (s_pv, 0)])
        for tt in range(4):
            dma("sync", xr_s[tt], xT_p[cc][:, tt * 512:(tt + 1) * 512],
                s_p3l, waits=[(s_x1, p3_xr_free[tt]),
                              (s_rn, rn_all) if (cc == 0 and tt == 0)
                              else (s_rn, 0)])
        xr_need = s_p3l.n

        bks3 = [(cc * 4 + b) % 6 for b in range(4)]

        def p3mm(t, par=par, w_tgt=w_tgts3[cc], bks3=tuple(bks3),
                 bf=tuple(p3_bank_free[bk] for bk in bks3)):
            t.wait_ge(s_wsl.h, w_tgt)
            t.wait_ge(s_rn.h, rn_all)
            for f in bf:
                if f:
                    t.wait_ge(s_x1.h, f)
            for jc in range(32):
                lhsT = slabs[par][:, jc * 128:(jc + 1) * 128]
                for b in range(4):
                    rhs = arena[:, jc * 2048 + b * 512:
                                jc * 2048 + b * 512 + 512]
                    ins = t.matmul(ps[bks3[b]][:], lhsT, rhs,
                                   start=(jc == 0), stop=(jc == 31))
            ins.then_inc(s_p3m.h, 1)
        add("tensor", p3mm)
        mm_tgt = s_p3m.bump(1)
        p3_slab_free[par] = mm_tgt

        for b in range(4):
            xp = b % 2
            e_tgt = step(
                "vector",
                [(s_p3m, mm_tgt), (s_p3l, xr_need), (s_x1st, x1b_free[xp])],
                lambda v, bk=bks3[b], b=b, xp=xp: v.tensor_add(
                    scr2[:, xp * 512:(xp + 1) * 512], ps[bk][:], xr_s[b]),
                s_x1)
            p3_bank_free[bks3[b]] = e_tgt
            p3_xr_free[b] = e_tgt
            x1b_free[xp] = dma(
                "sync", x1d[cc][:, b * 512:(b + 1) * 512],
                scr2[:, xp * 512:(xp + 1) * 512], s_x1st,
                waits=[(s_x1, e_tgt)])
    x1st_all = s_x1st.n
    p3m_all = s_p3m.n

    dma("gpsimd", BfT[:], Bf_p, s_bfl, waits=[(s_sc, sc_all), (s_pv, pv_all)])
    bfl_all = s_bfl.n

    obuf_free = [0, 0]
    bc5_prev = 0
    sf5_prev = 0
    out_prev_half = 0
    for th in range(2):
        ht0 = th * 1024
        AO = 32768 * (th % 2)
        BO = 32768 - AO
        a2w = [(s_p3m, p3m_all)] if th == 0 else [(s_mm4, s_mm4.n)]
        a2_tgt0 = dma("gpsimd", slab0[:], A2_p[0].rearrange("p c e -> p (c e)"),
                      s_a2l, waits=a2w)
        a2_tgt1 = dma("gpsimd", slab1[:], A2_p[1].rearrange("p c e -> p (c e)"),
                      s_a2l)
        a2_need = s_a2l.n

        for cc in range(NCC):
            rw = []
            if cc < 2:
                rw.append((s_x1st, x1st_all))
                if th == 1:
                    rw.append((s_mm4, s_mm4.n))
                    rw.append((s_x2, s_x2.n))
            rl_tgt = dma("sync", arena[:, AO + cc * 1024:AO + (cc + 1) * 1024],
                         x1d[cc][:, ht0:ht0 + 1024], s_rl4, waits=rw)
            sqw = [(s_rl4, rl_tgt), (s_sf4, s_sf4.n)]
            if cc == 0 and sf5_prev:
                sqw.append((s_sf5, sf5_prev))
            sq_tgt = step(
                "scalar", sqw,
                lambda sc, cc=cc, AO=AO: sc.activation(
                    scr1[:], arena[:, AO + cc * 1024:AO + (cc + 1) * 1024],
                    AF.Square),
                s_sq4)

            def stmm(t, cc=cc, rl_tgt=rl_tgt, sq_tgt=sq_tgt,
                     bc5_prev=bc5_prev, AO=AO):
                t.wait_ge(s_rl4.h, rl_tgt)
                t.wait_ge(s_sq4.h, sq_tgt)
                if cc == 0 and bc5_prev:
                    t.wait_ge(s_sf5.h, bc5_prev)
                for tt in range(2):
                    t.matmul(ps[4 + tt][0:1, :], onescol[:],
                             arena[:, AO + cc * 1024 + tt * 512:
                                   AO + cc * 1024 + (tt + 1) * 512],
                             start=(cc == 0), stop=(cc == NCC - 1))
                for tt in range(2):
                    bank = ps[6] if tt == 0 else ps[3]
                    ins = t.matmul(bank[0:1, :], onescol[:],
                                   scr1[:, tt * 512:(tt + 1) * 512],
                                   start=(cc == 0), stop=(cc == NCC - 1))
                ins.then_inc(s_sf4.h, 1)
            add("tensor", stmm)
            s_sf4.bump(1)
        sf_need = s_sf4.n

        def fin1(v, sf_need=sf_need, out_prev=out_prev_half):
            v.wait_ge(s_sf4.h, sf_need)
            if out_prev:
                v.wait_ge(s_out.h, out_prev)
            v.tensor_copy(strow[0][:, 0:512], ps[4][0:1, :])
            v.tensor_copy(strow[0][:, 512:1024], ps[5][0:1, :])
            v.tensor_copy(strow[1][:, 0:512], ps[6][0:1, :])
            v.tensor_copy(strow[1][:, 512:1024], ps[3][0:1, :])
            v.tensor_scalar_mul(strow[0], strow[0], 1.0 / C)
            v.tensor_scalar_mul(strow[1], strow[1], 1.0 / C)
            v.tensor_mul(strow[2], strow[0], strow[0])
            v.tensor_sub(strow[1], strow[1], strow[2]).then_inc(s_sf4.h, 1)
        add("vector", fin1)
        s_sf4.bump(1)

        def fin1b(sc, tgt=s_sf4.n):
            sc.wait_ge(s_sf4.h, tgt)
            sc.activation(strow[2], strow[1], AF.Ln, bias=LN_EPS)
            sc.activation(strow[3], strow[2], AF.Exp, scale=-0.5).then_inc(
                s_sf4.h, 1)
        add("scalar", fin1b)
        s_sf4.bump(1)

        def fin1c(v, tgt=s_sf4.n):
            v.wait_ge(s_sf4.h, tgt)
            v.tensor_mul(strow[0], strow[0], strow[3]).then_inc(s_sf4.h, 1)
        add("vector", fin1c)
        s_sf4.bump(1)

        for which in range(2):
            src = strow[3] if which == 0 else strow[0]
            lhsv = onesr1f if which == 0 else negr1f
            for tt in range(2):
                rm_t = step(
                    "tensor", [(s_sf4, s_sf4.n)],
                    lambda t, src=src, tt=tt, lhsv=lhsv: t.matmul(
                        ps[6][:], lhsv[:], src[:, tt * 512:(tt + 1) * 512],
                        start=True, stop=True),
                    s_sf4)
                step("vector", [(s_sf4, rm_t)],
                     lambda v, which=which, tt=tt: v.tensor_copy(
                         bcb[:, which * 1024 + tt * 512:
                             which * 1024 + (tt + 1) * 512], ps[6][:]),
                     s_sf4)
        bc_need = s_sf4.n

        nm_tgts = []
        for cc in range(NCC):
            nm_tgts.append(step("vector", [(s_sf4, bc_need)],
                 lambda v, cc=cc, AO=AO, BO=BO: (
                     v.tensor_mul(scr1[:],
                                  arena[:, AO + cc * 1024:
                                        AO + (cc + 1) * 1024],
                                  bcb[:, 0:1024]),
                     v.tensor_add(arena[:, BO + cc * 1024:
                                        BO + (cc + 1) * 1024],
                                  scr1[:], bcb[:, 1024:2048]))[-1],
                 s_nm4))
        nm_all = s_nm4.n

        for et in range(2):
            def hmm(t, et=et, nm_tgts=tuple(nm_tgts), a2_need=a2_need,
                    BO=BO):
                t.wait_ge(s_a2l.h, a2_need)
                for cc in range(NCC):
                    if et == 0:
                        t.wait_ge(s_nm4.h, nm_tgts[cc])
                    sb = slabs[cc // 16]
                    lhsT = sb[:, (cc % 16) * 256 + et * 128:
                              (cc % 16) * 256 + (et + 1) * 128]
                    for tt in range(2):
                        rhs = arena[:, BO + cc * 1024 + tt * 512:
                                    BO + cc * 1024 + (tt + 1) * 512]
                        ins = t.matmul(ps[et * 2 + tt][:], lhsT, rhs,
                                       start=(cc == 0), stop=(cc == NCC - 1))
                ins.then_inc(s_hm4.h, 1)
            add("tensor", hmm)
            hm_tgt = s_hm4.bump(1)
            for tt in range(2):
                step("vector", [(s_hm4, hm_tgt)],
                     lambda v, et=et, tt=tt: v.tensor_scalar_mul(
                         hT[:, et * 1024 + tt * 512:
                            et * 1024 + (tt + 1) * 512],
                         ps[et * 2 + tt][:], wppb[:, et:et + 1]),
                     s_he4)
        he_all = s_he4.n
        hm_all = s_hm4.n

        p4_bank_free = [0, 0, 0]
        pend_st2 = []
        w_tgts4 = {}
        w_tgts4[0] = dma("gpsimd", slab0[:], W_p[0], s_wsl,
                         waits=[(s_hm4, hm_all), (s_bfl, bfl_all)])
        p4_slab_free = [0, 0]
        for oc in range(NOC):
            par = oc % 2
            if oc + 1 < NOC:
                w_tgts4[oc + 1] = dma(
                    "gpsimd", slabs[(oc + 1) % 2][:], W_p[oc + 1], s_wsl,
                    waits=[(s_mm4, p4_slab_free[(oc + 1) % 2]),
                           (s_hm4, hm_all) if oc + 1 == 1 else (s_hm4, 0)])

            bk4 = [(oc * 2 + tt) % 3 for tt in range(2)]
            for f_ in pend_st2:
                add("tensor", f_)
            pend_st2 = []

            def p4mm(t, oc=oc, par=par, w_tgt=w_tgts4[oc], he_all=he_all,
                     bk4=tuple(bk4), bf=tuple(p4_bank_free[b] for b in bk4),
                     BO=BO):
                t.wait_ge(s_wsl.h, w_tgt)
                t.wait_ge(s_he4.h, he_all)
                for f in bf:
                    if f:
                        t.wait_ge(s_x2.h, f)
                for cc in range(NCC):
                    lhsT = slabs[par][:, cc * 128:(cc + 1) * 128]
                    for tt in range(2):
                        rhs = arena[:, BO + cc * 1024 + tt * 512:
                                    BO + cc * 1024 + (tt + 1) * 512]
                        t.matmul(ps[bk4[tt]][:], lhsT, rhs,
                                 start=(cc == 0), stop=False)
                for et in range(2):
                    lhsT = BfT[:, et * 4096 + oc * 128:
                               et * 4096 + (oc + 1) * 128]
                    for tt in range(2):
                        t.matmul(ps[bk4[tt]][:], lhsT,
                                 hT[:, et * 1024 + tt * 512:
                                    et * 1024 + (tt + 1) * 512],
                                 start=False, stop=False)
                for tt in range(2):
                    ins = t.matmul(ps[bk4[tt]][:],
                                   biasb[0:1, oc * 128:(oc + 1) * 128],
                                   onesrow[:], start=False, stop=True)
                ins.then_inc(s_mm4.h, 1)
            add("tensor", p4mm)
            mm_tgt = s_mm4.bump(1)
            p4_slab_free[par] = mm_tgt

            for tt in range(2):
                e_tgt = step(
                    "vector", [(s_mm4, mm_tgt)],
                    lambda v, oc=oc, tt=tt, bk=bk4[tt], AO=AO, BO=BO:
                    v.tensor_add(
                        arena[:, AO + oc * 1024 + tt * 512:
                              AO + oc * 1024 + (tt + 1) * 512],
                        ps[bk][:],
                        arena[:, BO + oc * 1024 + tt * 512:
                              BO + oc * 1024 + (tt + 1) * 512]),
                    s_x2)
                p4_bank_free[bk4[tt]] = e_tgt
                sq_tgt = step(
                    "scalar", [(s_x2, e_tgt), (s_sf5, s_sf5.n)],
                    lambda sc, oc=oc, tt=tt, AO=AO: sc.activation(
                        scr1[:, tt * 512:(tt + 1) * 512],
                        arena[:, AO + oc * 1024 + tt * 512:
                              AO + oc * 1024 + (tt + 1) * 512],
                        AF.Square),
                    s_sq5)

                def st2mm(t, oc=oc, tt=tt, e_tgt=e_tgt, sq_tgt=sq_tgt,
                          AO=AO):
                    t.wait_ge(s_x2.h, e_tgt)
                    t.wait_ge(s_sq5.h, sq_tgt)
                    t.matmul(ps[4 + tt][0:1, :], onescol[:],
                             arena[:, AO + oc * 1024 + tt * 512:
                                   AO + oc * 1024 + (tt + 1) * 512],
                             start=(oc == 0), stop=(oc == NOC - 1))
                    bank = ps[6] if tt == 0 else ps[3]
                    t.matmul(bank[0:1, :], onescol[:],
                             scr1[:, tt * 512:(tt + 1) * 512],
                             start=(oc == 0), stop=(oc == NOC - 1)
                             ).then_inc(s_sf5.h, 1)
                pend_st2.append(st2mm)
                s_sf5.bump(1)
        for f_ in pend_st2:
            add("tensor", f_)
        pend_st2 = []
        x2_prev_half = s_x2.n
        sf5_need = s_sf5.n

        def fin2(v, sf5_need=sf5_need):
            v.wait_ge(s_sf5.h, sf5_need)
            v.tensor_copy(strow[0][:, 0:512], ps[4][0:1, :])
            v.tensor_copy(strow[0][:, 512:1024], ps[5][0:1, :])
            v.tensor_copy(strow[1][:, 0:512], ps[6][0:1, :])
            v.tensor_copy(strow[1][:, 512:1024], ps[3][0:1, :])
            v.tensor_scalar_mul(strow[0], strow[0], 1.0 / C)
            v.tensor_scalar_mul(strow[1], strow[1], 1.0 / C)
            v.tensor_mul(strow[2], strow[0], strow[0])
            v.tensor_sub(strow[1], strow[1], strow[2]).then_inc(s_sf5.h, 1)
        add("vector", fin2)
        s_sf5.bump(1)

        def fin2b(sc, tgt=s_sf5.n):
            sc.wait_ge(s_sf5.h, tgt)
            sc.activation(strow[2], strow[1], AF.Ln, bias=LN_EPS)
            sc.activation(strow[3], strow[2], AF.Exp, scale=-0.5).then_inc(
                s_sf5.h, 1)
        add("scalar", fin2b)
        s_sf5.bump(1)

        def fin2c(v, tgt=s_sf5.n):
            v.wait_ge(s_sf5.h, tgt)
            v.tensor_mul(strow[0], strow[0], strow[3]).then_inc(s_sf5.h, 1)
        add("vector", fin2c)
        s_sf5.bump(1)

        for which in range(2):
            src = strow[3] if which == 0 else strow[0]
            lhsv = onesr1f if which == 0 else negr1f
            for tt in range(2):
                rm_t = step(
                    "tensor", [(s_sf5, s_sf5.n)],
                    lambda t, src=src, tt=tt, lhsv=lhsv: t.matmul(
                        ps[6][:], lhsv[:], src[:, tt * 512:(tt + 1) * 512],
                        start=True, stop=True),
                    s_sf5)
                step("vector", [(s_sf5, rm_t)],
                     lambda v, which=which, tt=tt: v.tensor_copy(
                         bcb[:, which * 1024 + tt * 512:
                             which * 1024 + (tt + 1) * 512], ps[6][:]),
                     s_sf5)
        bc5_need = s_sf5.n

        no5_tgts = []
        for oc in range(NOC):
            obp = oc % 2
            no_tgt = step(
                "vector",
                [(s_sf5, bc5_need), (s_out, obuf_free[obp])],
                lambda v, oc=oc, obp=obp, AO=AO: (
                    v.tensor_mul(ob_s[obp],
                                 arena[:, AO + oc * 1024:
                                       AO + (oc + 1) * 1024],
                                 bcb[:, 0:1024]),
                    v.tensor_add(ob_s[obp], ob_s[obp],
                                 bcb[:, 1024:2048]))[-1],
                s_no5)
            no5_tgts.append(no_tgt)
            obuf_free[obp] = dma(
                "sync", out_d[oc * 128:(oc + 1) * 128, ht0:ht0 + 1024],
                ob_s[obp], s_out, waits=[(s_no5, no_tgt)])
        bc5_prev = bc5_need
        sf5_prev = sf5_need
        out_prev_half = s_out.n

    out_all = s_out.n

    def fin(s):
        s.wait_ge(s_out.h, out_all)
    add("sync", fin)

    with nc.Block() as block:
        @block.sync
        def _(en):
            rt["sync_role"] = nc.partition_id(engines=[ET.SP]) % 2
            for fn in ops["sync"]:
                fn(en)

        @block.gpsimd
        def _(en):
            pid = nc.partition_id(engines=[ET.Pool])
            rt["gp_opp"] = (pid + 1) % 2
            for fn in ops["gpsimd"]:
                fn(en)

        @block.tensor
        def _(en):
            for fn in ops["tensor"]:
                fn(en)

        @block.vector
        def _(en):
            for fn in ops["vector"]:
                fn(en)

        @block.scalar
        def _(en):
            for fn in ops["scalar"]:
                fn(en)

    ctx.close()
    return nc


_NC_CACHE = [None]


def _get_nc():
    if _NC_CACHE[0] is None:
        _NC_CACHE[0] = _build_nc()
    return _NC_CACHE[0]


def _host_inputs(x, wq, wk, wv, wo, W, bias, A, Bf, coeffs, scales):
    x = np.asarray(x)

    def perm4(w):
        return np.ascontiguousarray(
            np.asarray(w).T.reshape(NCC, 128, NOC, 128).transpose(2, 1, 0, 3))

    wq_p, wk_p, wv_p = perm4(wq), perm4(wk), perm4(wv)
    wo_p, W_p = perm4(wo), perm4(W)
    A2 = np.asarray(A).reshape(256, C)
    A2T = np.ascontiguousarray(A2.T)
    A2_p = np.ascontiguousarray(
        A2T.reshape(2, 16, 128, 256).transpose(0, 2, 1, 3))
    Bf_re = np.ascontiguousarray(
        np.asarray(Bf).transpose(0, 2, 1).reshape(256, C))
    Bf_p = np.ascontiguousarray(Bf_re.reshape(2, 128, C).transpose(1, 0, 2))
    bias_in = np.ascontiguousarray(np.asarray(bias).reshape(1, C))
    coef_in = np.ascontiguousarray(
        np.asarray(coeffs).reshape(16, 1).astype(np.float32))
    scal_in = np.ascontiguousarray(
        np.asarray(scales).reshape(16, 1).astype(np.float32))
    m16 = np.zeros((16, 256), dtype=np.float32)
    for ee in range(16):
        m16[ee, ee * 16:(ee + 1) * 16] = 1.0
    ident = np.eye(128, dtype=np.float32).astype(bfdt)
    nonce = np.array([[np.random.randint(1, 2**31 - 1)]], dtype=np.uint32)

    maps = []
    for pid in range(8):
        b, r = pid // 2, pid % 2
        own_blocks = np.arange(r, 32, 2)
        own_tok = (own_blocks[:, None] * 128 + np.arange(128)[None, :]).ravel()
        xb = np.ascontiguousarray(x[b][own_tok, :])
        xT = np.ascontiguousarray(xb.T.reshape(NCC, 128, TOK))
        gpos = own_tok.astype(np.float32)
        cosr = np.ascontiguousarray(
            np.broadcast_to(np.cos(gpos)[None, :], (128, TOK))
        ).astype(np.float32)
        sgn = np.where(np.arange(128) % 2 == 0, -1.0, 1.0)[:, None]
        salr = (sgn * np.sin(gpos)[None, :]).astype(np.float32)
        mask = np.zeros((128, 8, 512), dtype=np.float32)
        qq = np.arange(512)
        qrel = 2 * (qq // 128) * 128 + r * 128 + (qq % 128)
        for d in range(8):
            kk = np.arange(128)[:, None]
            mask[:, d, :] = np.where((d * 128 + kk) > qrel[None, :],
                                     -1e30, 0.0)
        maps.append({
            "xT_p": xT.astype(np.float32),
            "wq_p": wq_p, "wk_p": wk_p, "wv_p": wv_p, "wo_p": wo_p,
            "W_p": W_p, "A2_p": A2_p, "Bf_p": Bf_p,
            "bias_in": bias_in.astype(np.float32),
            "coef_in": coef_in, "scal_in": scal_in, "m16_in": m16,
            "cos_in": cosr, "sal_in": salr,
            "mask_in": np.ascontiguousarray(
                mask.reshape(128, 4096)).astype(bfdt),
            "ident_in": ident,
            "nonce_in": nonce,
        })
    return maps


def kernel(L0_attn_x, L0_attn_wq, L0_attn_wk, L0_attn_wv, L0_attn_wo,
           L0_ffn_W, L0_ffn_bias, L0_ffn_A, L0_ffn_Bf, L0_ffn_coeffs,
           L0_ffn_scales):
    nc = _get_nc()
    in_maps = _host_inputs(L0_attn_x, L0_attn_wq, L0_attn_wk, L0_attn_wv,
                           L0_attn_wo, L0_ffn_W, L0_ffn_bias, L0_ffn_A,
                           L0_ffn_Bf, L0_ffn_coeffs, L0_ffn_scales)
    res = run_bass_kernel_spmd(nc, in_maps, core_ids=list(range(8)))
    out = np.zeros((B, T, C), dtype=np.float32)
    for pid in range(8):
        b, r = pid // 2, pid % 2
        own_blocks = np.arange(r, 32, 2)
        own_tok = (own_blocks[:, None] * 128 + np.arange(128)[None, :]).ravel()
        out[b][own_tok, :] = res.results[pid]["out_fm"].T
    return out
